# version 09
# baseline (speedup 1.0000x reference)
import os
import numpy as np
import ml_dtypes
from contextlib import ExitStack

import concourse.bass as bass
import concourse.mybir as mybir
from concourse.bass_utils import run_bass_kernel_spmd

F32 = mybir.dt.float32
BF16 = mybir.dt.bfloat16
AF = mybir.ActivationFunctionType
ALU = mybir.AluOpType
AX = mybir.AxisListType

NCORES = 8
S = 2
T = 2048
D = 1024
DCH = 8
H = 16
DH = 64
QCW = 512
NQC = T // QCW
NKT = T // 128
TKW = 256
NTK = T // TKW
I_EXP = 512
NEG_BIG = -1e30
ENGS = ("sync", "gpsimd", "tensor", "scalar", "vector")


def _bfa(x):
    return np.ascontiguousarray(x).astype(ml_dtypes.bfloat16)


def _f(x):
    return np.ascontiguousarray(x, dtype=np.float32)


def host_layout(inputs):
    g = {}
    x = _f(inputs["L0_attn_x"])
    g["xT"] = np.ascontiguousarray(x.transpose(0, 2, 1))

    def wT(w):
        return _f(np.ascontiguousarray(np.asarray(w).T).reshape(DCH, 128, -1))

    def bias_col(b):
        return _f(np.ascontiguousarray(np.asarray(b).reshape(DCH, 128).T))

    perm = np.concatenate([np.concatenate([h * 64 + np.arange(0, 64, 2),
                                           h * 64 + np.arange(1, 64, 2)])
                           for h in range(H)])

    g["l0_wq"] = wT(inputs["L0_attn_Wq"]); g["l0_bq"] = bias_col(inputs["L0_attn_bq"])
    g["l0_wk"] = wT(inputs["L0_attn_Wk"]); g["l0_bk"] = bias_col(inputs["L0_attn_bk"])
    g["l0_wv"] = wT(inputs["L0_attn_Wv"]); g["l0_bv"] = _f(inputs["L0_attn_bv"]).reshape(1, D)
    g["l0_wo"] = wT(inputs["L0_attn_Wo"]); g["l0_bo"] = bias_col(inputs["L0_attn_bo"])
    wq1 = np.asarray(_f(inputs["L1_attn_Wq"]))[perm]
    wk1 = np.asarray(_f(inputs["L1_attn_Wk"]))[perm]
    g["l1_wq"] = wT(wq1); g["l1_bq"] = bias_col(np.asarray(_f(inputs["L1_attn_bq"]))[perm])
    g["l1_wk"] = wT(wk1); g["l1_bk"] = bias_col(np.asarray(_f(inputs["L1_attn_bk"]))[perm])
    g["l1_wv"] = wT(inputs["L1_attn_Wv"]); g["l1_bv"] = _f(inputs["L1_attn_bv"]).reshape(1, D)
    g["l1_wp"] = wT(inputs["L1_attn_Wp"]); g["l1_bp"] = bias_col(inputs["L1_attn_bp"])

    g["l0_gate"] = _f(np.ascontiguousarray(np.asarray(_f(inputs["L0_ffn_gate_w"])).T).reshape(DCH, 128, 8))
    g["l0_gbias"] = _f(inputs["L0_ffn_lb_bias"]).reshape(8, 1)
    g["l1_gate"] = _f(np.ascontiguousarray(_f(inputs["L1_ffn_gate_w"])).reshape(DCH, 128, 16))
    g["l1_gbias"] = _f(inputs["L1_ffn_gate_b"]).reshape(16, 1)

    e0g = np.ascontiguousarray(_f(inputs["L0_ffn_expert_gate_w"]).transpose(0, 2, 1))
    e0u = np.ascontiguousarray(_f(inputs["L0_ffn_expert_up_w"]).transpose(0, 2, 1))
    g["e0_gu"] = _f(np.stack([e0g, e0u], axis=1).reshape(8, 2, DCH, 128, I_EXP))
    e0d = np.ascontiguousarray(_f(inputs["L0_ffn_expert_down_w"]).transpose(0, 2, 1))
    g["e0_d"] = _f(e0d.reshape(8, 4, 128, D))
    g["s0_g"] = wT(inputs["L0_ffn_shared_gate_w"])
    g["s0_u"] = wT(inputs["L0_ffn_shared_up_w"])
    g["s0_d"] = _f(np.ascontiguousarray(_f(inputs["L0_ffn_shared_down_w"]).T).reshape(DCH, 128, D))

    g["e1_gu"] = _f(np.stack([_f(inputs["L1_ffn_experts_wg"]),
                              _f(inputs["L1_ffn_experts_wu"])], axis=1).reshape(16, 2, DCH, 128, I_EXP))
    g["e1_d"] = _f(_f(inputs["L1_ffn_experts_wd"]).reshape(16, 4, 128, D))
    g["s1_gu"] = _f(np.stack([_f(inputs["L1_ffn_shared_wg"]),
                              _f(inputs["L1_ffn_shared_wu"])], axis=0).reshape(2, DCH, 128, I_EXP))
    g["s1_d"] = _f(_f(inputs["L1_ffn_shared_wd"]).reshape(4, 128, D))

    j = np.arange(32, dtype=np.float64)
    invf = 1.0 / (10000.0 ** (2.0 * j / 64.0))
    t = np.arange(T, dtype=np.float64)
    ang = t[None, :] * invf[:, None]
    cos = np.cos(ang); sin = np.sin(ang)
    g["cosin"] = _f(np.stack([np.concatenate([cos, cos], axis=0),
                              np.concatenate([-sin, sin], axis=0),
                              np.concatenate([sin, -sin], axis=0)], axis=0))
    g["eye"] = _f(np.eye(128))
    sel = np.zeros((16, 16, 128), np.float32)
    for e in range(16):
        sel[e, e, :] = 1.0
    g["selbank"] = _f(sel.transpose(1, 0, 2).reshape(16, 16 * 128))
    i_ = np.arange(128)[:, None]; j_ = np.arange(512)[None, :]
    mk = np.stack([(j_ >= i_ + m * 128) for m in range(4)], axis=0).astype(np.float32)
    g["masks"] = _f(np.concatenate([mk, mk], axis=2))
    return g


class Pipe:
    def __init__(self, st, nc, name):
        self.sem = st.enter_context(nc.semaphore(name))
        self.count = 0
        self.name = name


class Bank:
    def __init__(self, st, nc, idx, cols=512):
        self.idx = idx
        self.t = st.enter_context(nc.psum_tensor(
            f"pbank{idx}_{nc.next_id()}", [128, cols], F32, unsafe_allow_realloc=True))
        self.mm_tok = None
        self.free_tok = None


class Build:

    def __init__(self, nc, st, debug_names=(), serialize=False):
        self.nc = nc
        self.st = st
        self.debug_names = set(debug_names)
        self.dbg = {}
        self.q = {e: [] for e in ENGS}
        self.tail_signed = {e: True for e in ENGS}
        self.serialize = serialize

    def snap(self, names):
        from types import SimpleNamespace
        return SimpleNamespace(**{n: getattr(self, n) for n in names.split()})

    def E(self, engname, fn):
        self.q[engname].append(("op", fn, None, 0))
        self.tail_signed[engname] = False

    def sig(self, engname, fn):
        pipe = self.pp[engname]
        pipe.count += 1
        self.q[engname].append(("op", fn, pipe, 1))
        self.tail_signed[engname] = True
        return (pipe, pipe.count)

    def dma(self, engname, fn, pipe):
        pipe.count += 16
        self.q[engname].append(("op", fn, pipe, 16))
        self.tail_signed[engname] = True
        return (pipe, pipe.count)

    def see(self, engname, tok):
        if tok is not None:
            self.q[engname].append(("wait", tok[0], tok[1]))

    def W(self, engname, pipe, count=None):
        c = pipe.count if count is None else count
        if c > 0:
            self.q[engname].append(("wait", pipe, c))

    def drain(self, engname, eng):
        for item in self.q[engname]:
            if item[0] == "wait":
                _, pipe, c = item
                eng.wait_ge(pipe.sem, c)
            else:
                _, fn, produce, n = item
                instr = fn(eng)
                if produce is not None:
                    instr.then_inc(produce.sem, n)

    def declare_io(self):
        nc = self.nc
        def din(name, shape):
            return nc.dram_tensor(name, list(shape), F32, kind="ExternalInput").ap()
        self.i = {}
        for nm, shape in [
            ("xT", [S, D, T]),
            ("l0_wq", [DCH, 128, D]), ("l0_wk", [DCH, 128, D]), ("l0_wv", [DCH, 128, D]),
            ("l0_wo", [DCH, 128, D]), ("l1_wq", [DCH, 128, D]), ("l1_wk", [DCH, 128, D]),
            ("l1_wv", [DCH, 128, D]), ("l1_wp", [DCH, 128, D]),
            ("l0_bq", [128, 8]), ("l0_bk", [128, 8]), ("l0_bo", [128, 8]),
            ("l1_bq", [128, 8]), ("l1_bk", [128, 8]), ("l1_bp", [128, 8]),
            ("l0_bv", [1, D]), ("l1_bv", [1, D]),
            ("l0_gate", [DCH, 128, 8]), ("l0_gbias", [8, 1]),
            ("l1_gate", [DCH, 128, 16]), ("l1_gbias", [16, 1]),
            ("e0_gu", [8, 2, DCH, 128, I_EXP]), ("e0_d", [8, 4, 128, D]),
            ("s0_g", [DCH, 128, D]), ("s0_u", [DCH, 128, D]), ("s0_d", [DCH, 128, D]),
            ("e1_gu", [16, 2, DCH, 128, I_EXP]), ("e1_d", [16, 4, 128, D]),
            ("s1_gu", [2, DCH, 128, I_EXP]), ("s1_d", [4, 128, D]),
            ("cosin", [3, 64, T]), ("eye", [128, 128]), ("selbank", [16, 16 * 128]),
            ("masks", [4, 128, 1024]),
        ]:
            self.i[nm] = din(nm, shape)
        self.out = nc.dram_tensor("out", [S, D, T], F32, kind="ExternalOutput").ap()
        k = lambda nm: "ExternalOutput" if nm in self.debug_names else "Internal"
        self.xn_scr = nc.dram_tensor("xn_scr", [S, NQC, 128, DCH * QCW], BF16, kind=k("xn_scr")).ap()
        self.x1_scr = nc.dram_tensor("x1_scr", [S, NTK, 128, DCH * TKW], F32, kind=k("x1_scr")).ap()
        self.ao_scr = nc.dram_tensor("ao_scr", [NQC, 128, DCH * QCW], BF16, kind=k("ao_scr")).ap()
        self.cc_in = nc.dram_tensor("cc_in", [128, 32], F32)
        self.cc_out = nc.dram_tensor("cc_out", [128, 32], F32)

    def dout(self, name, shape, dt=F32):
        if name in self.debug_names:
            self.dbg[name] = self.nc.dram_tensor("dbg_" + name, list(shape), dt,
                                                 kind="ExternalOutput").ap()
            return self.dbg[name]
        return None

    def alloc(self):
        nc, st = self.nc, self.st
        sb = lambda name, shape, dt: st.enter_context(nc.sbuf_tensor(name, shape, dt))
        self.cos = sb("cos", [128, T], BF16)
        self.ssin = sb("ssin", [128, T], BF16)
        self.ssin2 = sb("ssin2", [128, T], BF16)
        self.eye_f = sb("eye_f", [128, 128], F32)
        self.eye_b = sb("eye_b", [128, 128], BF16)
        self.nlb0 = sb("nlb0", [8, 1], F32)
        self.nlb1 = sb("nlb1", [16, 1], F32)
        self.bias_s = {}
        for nm in ["l0_bq", "l0_bk", "l0_bo", "l1_bq", "l1_bk", "l1_bp"]:
            self.bias_s[nm] = sb(nm + "_s", [128, 8], F32)
        self.ones_b = sb("ones_b", [128, 1], BF16)
        self.ones_f = sb("ones_f", [128, 1], F32)
        self.ones_r = sb("ones_r", [1, 128], F32)
        self.ones_rb = sb("ones_rb", [1, 128], BF16)
        self.eps_ln = sb("eps_ln", [1, 1], F32)
        self.w0T = [sb(f"w0T{s}", [16, T], BF16) for s in range(S)]
        self.w1T = [sb(f"w1T{s}", [16, T], BF16) for s in range(S)]
        self.msel_st = [sb(f"msel{s}", [128, 16 * 8], F32) for s in range(S)]
        self.stats_loc = sb("stats_loc", [128, 32], F32)
        self.stats_gl = sb("stats_gl", [8, 2], F32)
        self.wmean = sb("wmean", [8, 1], F32)
        self.wm_c = sb("wm_c", [8, 1], F32)
        self.wm_rc = sb("wm_rc", [8, 1], F32)
        self.wslot = [sb(f"wslot{k}", [128, 8192], BF16) for k in range(3)]
        self.qmark = sb("qmark", [1, 2], F32)
        self.wslot_free = [None, None, None]
        self.wslot_tok = [None, None, None]
        P = lambda name: Pipe(st, nc, name)
        self.p_const = P("p_const")
        self.p_w = P("p_w")
        self.p_xin = P("p_xin")
        self.p_xn_st = P("p_xn_st")
        self.p_xn_ld = P("p_xn_ld")
        self.p_res = P("p_res")
        self.p_out = P("p_out")
        self.p_ao = P("p_ao")
        self.p_cc = P("p_cc")
        self.p_ccc = P("p_ccc")
        self.p_dbg = P("p_dbg")
        self.pp = {e: P("pp_" + e) for e in ENGS}
        self.tok = {}

    def quiesce(self):
        if not self.tail_signed["vector"]:
            self.sig("vector", lambda v: v.memset(self.qmark[0:1, 0:1], 0.0))
        if not self.tail_signed["scalar"]:
            self.sig("scalar", lambda se: se.activation(self.qmark[0:1, 1:2],
                                                        self.qmark[0:1, 0:1], AF.Copy))
        assert self.tail_signed["tensor"], "tensor queue tail must be signalled"
        assert self.tail_signed["gpsimd"], "gpsimd queue tail must be signalled"
        pipes = [self.p_const, self.p_w, self.p_xin, self.p_xn_st, self.p_xn_ld,
                 self.p_res, self.p_out, self.p_ao, self.p_cc, self.p_ccc,
                 self.p_dbg] + [self.pp[e] for e in ENGS]
        for e in ENGS:
            for p in pipes:
                self.W(e, p)

    def open_scope(self, alloc_fn):
        st2 = ExitStack()
        self._scopes = getattr(self, "_scopes", [])
        self._scopes.append(st2)
        alloc_fn(st2)

    def close_scope(self):
        self.quiesce()
        if os.environ.get("BIG_SBUF") == "1":
            pass
        else:
            self._scopes.pop().close()

    def alloc_pre(self, st):
        nc = self.nc
        sb = lambda name, shape, dt: st.enter_context(nc.sbuf_tensor(
            name + str(nc.next_id()), shape, dt, unsafe_allow_realloc=True))
        self.banks = [Bank(st, nc, i) for i in range(8)]
        g0f = self.g0f = sb("g0f", [128, DCH * 8], F32)
        g1f = self.g1f = sb("g1f", [128, DCH * 16], F32)
        self.dma("gpsimd", lambda g, g0f=g0f: g.dma_start(
            out=g0f[:, :].rearrange("p (c e) -> p c e", c=DCH),
            in_=self.i["l0_gate"].transpose([1, 0, 2])), self.p_const)
        self.dma("gpsimd", lambda g, g1f=g1f: g.dma_start(
            out=g1f[:, :].rearrange("p (c e) -> p c e", c=DCH),
            in_=self.i["l1_gate"].transpose([1, 0, 2])), self.p_const)
        for e in ENGS:
            self.W(e, self.p_const)
        self.xc = [sb("xc", [128, DCH * QCW], F32) for p in range(2)]
        self.xq = sb("xq", [128, DCH * QCW], BF16)
        self.xnb = sb("xnb", [128, DCH * QCW], BF16)
        self.xnf = sb("xnf", [128, DCH * QCW], F32)
        self.mrow = sb("mrow", [1, QCW], F32)
        self.erow = sb("erow", [1, QCW], F32)
        self.vrow = sb("vrow", [1, QCW], F32)
        self.rrow = sb("rrow", [1, QCW], F32)
        self.tmp512 = sb("tmp512", [128, QCW], F32)
        self.tmp512b = sb("tmp512b", [128, QCW], F32)
        self.sct = sb("sct", [16, QCW], F32)
        self.sct2 = sb("sct2", [16, QCW], F32)
        self.sctok4 = sb("sctok4", [128, 4 * 16], F32)
        self.t2 = {}
        for nm in ["m1", "m2", "s2", "w1", "w2", "wtok"]:
            self.t2[nm] = sb("t2" + nm, [128, 4 * 16], F32)
        for nm in ["max1", "max2", "den", "rden", "a1", "a2"]:
            self.t2[nm] = sb("t2" + nm, [128, 4], F32)

    def alloc_att(self, st):
        nc = self.nc
        sb = lambda name, shape, dt: st.enter_context(nc.sbuf_tensor(
            name + str(nc.next_id()), shape, dt, unsafe_allow_realloc=True))
        self.abank = [Bank(st, nc, 100 + i) for i in range(4)]
        self.psc2 = [Bank(st, nc, 200 + p, cols=1024) for p in range(2)]
        maskb = self.maskb = sb("maskb", [128, 4 * 1024], BF16)
        self.dma("gpsimd", lambda g, maskb=maskb: g.dma_start(
            out=maskb[:, :].rearrange("p (m w) -> p m w", m=4),
            in_=self.i["masks"].transpose([1, 0, 2])), self.p_const)
        for e in ("vector", "scalar", "tensor"):
            self.W(e, self.p_const)
        self.KT = sb("KT", [128, DCH * T], BF16)
        self.Vp = sb("Vp", [128, NKT * (H * 65)], BF16)
        self.QTc = sb("QTc", [128, DCH * QCW], BF16)
        self.och = sb("och", [128, DCH * QCW], BF16)
        self.xn_ch = [sb("xn_ch", [128, DCH * QCW], BF16) for p in range(2)]
        self.bvrow = sb("bvrow", [1, D], BF16)
        self.bvb = sb("bvb", [128, D], BF16)
        self.k_bf = sb("k_bf", [128, QCW], BF16)
        self.q_bf = sb("q_bf", [128, QCW], BF16)
        self.rc1 = sb("rc1", [128, QCW], BF16)
        self.rc2 = sb("rc2", [128, QCW], BF16)
        self.exp2 = [sb("exp2", [128, 2 * QCW], BF16) for p in range(2)]
        self.oraw = sb("oraw", [65, 4 * QCW], BF16)
        self.lrow = [sb("lrow", [1, QCW], F32) for _ in range(2)]
        self.lrinv = [sb("lrinv", [1, QCW], BF16) for _ in range(2)]
        self.linv_b = sb("linv_b", [64, 2 * QCW], BF16)

    def alloc_ffn(self, st):
        nc = self.nc
        sb = lambda name, shape, dt: st.enter_context(nc.sbuf_tensor(
            name + str(nc.next_id()), shape, dt, unsafe_allow_realloc=True))
        self.fbank = [Bank(st, nc, 300 + i) for i in range(8)]
        selb = self.selb = sb("selb", [16, 16 * 128], BF16)
        self.dma("gpsimd", lambda g, selb=selb: g.dma_start(out=selb[:, :], in_=self.i["selbank"]),
                 self.p_const)
        for e in ("vector", "scalar", "tensor"):
            self.W(e, self.p_const)
        self.xn_s = sb("xn_s", [128, DCH * T], BF16)
        self.acc = sb("acc", [128, DCH * T], F32)
        self.att_tk = [sb("att_tk", [128, DCH * TKW], BF16)]
        self.xres_tk = [sb("xres_tk", [128, DCH * TKW], F32)]
        self.m_buf = [sb("m_buf", [128, 8 * TKW], BF16) for p in range(2)]
        self.wbc_sb = sb("wbc_sb", [128, 2 * 2 * TKW], BF16)
        self.sg_sb = sb("sg_sb", [128, 2 * TKW], BF16)
        self.su_sb = sb("su_sb", [128, 2 * TKW], BF16)

    def load_consts(self):
        i = self.i
        skip = os.environ.get("SKIP_CONSTS") == "1"
        def d(out, in_):
            if skip:
                return
            self.dma("gpsimd", lambda g, out=out, in_=in_: g.dma_start(out=out, in_=in_),
                     self.p_const)
        d(self.cos[0:64, :], i["cosin"][0])
        d(self.cos[64:128, :], i["cosin"][0])
        d(self.ssin[0:64, :], i["cosin"][1])
        d(self.ssin[64:128, :], i["cosin"][1])
        d(self.ssin2[0:64, :], i["cosin"][2])
        d(self.ssin2[64:128, :], i["cosin"][2])
        d(self.eye_f[:, :], i["eye"])
        d(self.eye_b[:, :], i["eye"])
        d(self.nlb0[:, :], i["l0_gbias"])
        d(self.nlb1[:, :], i["l1_gbias"])
        for nm in self.bias_s:
            d(self.bias_s[nm][:, :], i[nm])
        for tile, val in [(self.ones_b, 1.0), (self.ones_f, 1.0), (self.ones_r, 1.0),
                          (self.ones_rb, 1.0),
                          (self.eps_ln, 1e-5), (self.stats_loc, 0.0),
                          (self.w0T[0], 0.0), (self.w0T[1], 0.0), (self.w1T[0], 0.0), (self.w1T[1], 0.0)]:
            self.E("vector", lambda ve, t=tile, v=val: ve.memset(t[:, :], v))
        self.W("vector", self.p_const)
        self.E("vector", lambda v: v.tensor_scalar_mul(self.nlb0[:, :], self.nlb0[:, :], -1.0))
        tok = self.sig("vector",
                       lambda v: v.tensor_scalar_mul(self.nlb1[:, :], self.nlb1[:, :], -1.0))
        for e in ENGS:
            self.W(e, self.p_const)
            if e != "vector":
                self.see(e, tok)

    def prepass_seq(self, layer, s):
        SN = self.snap("xc xq xnb xnf mrow erow vrow rrow tmp512 tmp512b sct sct2 sctok4 t2")
        """LN stats -> xn (fp32 + bf16, stored to HBM) -> fp32 router sigmoid ->
        top-2. layer 0: accumulate global (S, C) stats + keep msel; layer 1:
        build w1T. Same-engine dependent DVE/ACT pairs are guarded with tokens."""
        E, sig, see, W = self.E, self.sig, self.see, self.W
        Elay = 8 if layer == 0 else 16
        nlb = self.nlb0 if layer == 0 else self.nlb1
        gatef = self.g0f if layer == 0 else self.g1f
        bS, bE, bM, bR, bB0, bB1, bT, bW_ = self.banks
        t2 = SN.t2
        tk = self.tok

        for tc in range(NQC):
            par = tc % 2
            xcp = SN.xc[par]
            see("sync", tk.get(("xc_free", par)))
            if layer == 0:
                src = self.i["xT"][s].rearrange("(c p) t -> p c t", p=128)[:, :, tc * QCW:(tc + 1) * QCW]
                self.dma("sync", lambda sy, src=src, xcp=xcp: sy.dma_start(
                    out=xcp[:, :].rearrange("p (c t) -> p c t", c=DCH), in_=src), self.p_xin)
            else:
                for h2 in range(2):
                    src = self.x1_scr[s, tc * 2 + h2].rearrange("p (c u) -> p c u", c=DCH)
                    dst = xcp[:, :].rearrange("p (c t) -> p c t", c=DCH)[:, :, h2 * TKW:(h2 + 1) * TKW]
                    self.dma("sync", lambda sy, src=src, dst=dst: sy.dma_start(out=dst, in_=src),
                             self.p_xin)
            W("scalar", self.p_xin)
            see("scalar", tk.get("xq_free"))
            tok_sc = sig("scalar", lambda se, xcp=xcp: se.activation(SN.xq[:, :], xcp[:, :], AF.Square))
            W("tensor", self.p_xin)
            see("tensor", bS.free_tok)
            for d in range(DCH):
                fs = (d == 0); ls = (d == DCH - 1)
                mS = lambda te, d=d, fs=fs, ls=ls, xcp=xcp: te.matmul(
                    bS.t[0:1, 0:QCW], self.ones_f[:, 0:1],
                    xcp[:, d * QCW:(d + 1) * QCW], start=fs, stop=ls)
                if ls:
                    tok_mmS = sig("tensor", mS)
                else:
                    E("tensor", mS)
            see("tensor", tok_sc)
            see("tensor", bE.free_tok)
            for d in range(DCH):
                fs = (d == 0); ls = (d == DCH - 1)
                mE = lambda te, d=d, fs=fs, ls=ls: te.matmul(
                    bE.t[0:1, 0:QCW], self.ones_b[:, 0:1],
                    SN.xq[:, d * QCW:(d + 1) * QCW], start=fs, stop=ls)
                if ls:
                    tok_mm = sig("tensor", mE)
                else:
                    E("tensor", mE)
            bS.mm_tok = tok_mmS
            bE.mm_tok = tok_mm
            tk["xq_free"] = tok_mm
            see("vector", tok_mmS)
            see("vector", tok_mm)
            see("vector", tk.get("rows_free"))
            tok_a = sig("vector", lambda v: v.tensor_scalar_mul(SN.mrow[:, :], bS.t[0:1, 0:QCW], 1.0 / D))
            tok_b = sig("vector", lambda v: v.tensor_scalar_mul(SN.erow[:, :], bE.t[0:1, 0:QCW], 1.0 / D))
            see("vector", tok_a)
            tok_c = sig("vector", lambda v: v.tensor_mul(SN.vrow[:, :], SN.mrow[:, :], SN.mrow[:, :]))
            see("vector", tok_c)
            tok_rows = sig("vector", lambda v: v.tensor_sub(SN.vrow[:, :], SN.erow[:, :], SN.vrow[:, :]))
            bS.free_tok = bE.free_tok = tok_rows
            see("scalar", tok_rows)
            tok_ln = sig("scalar", lambda se: se.activation(SN.rrow[:, :], SN.vrow[:, :], AF.Ln,
                                                            bias=self.eps_ln[0:1, 0:1]))
            see("scalar", tok_ln)
            tok_rstd = sig("scalar", lambda se: se.activation(SN.rrow[:, :], SN.rrow[:, :],
                                                              AF.Exp, scale=-0.5))
            see("tensor", tok_rstd)
            see("tensor", bB0.free_tok)
            see("tensor", bB1.free_tok)
            bB0.mm_tok = sig("tensor", lambda te: te.matmul(
                bB0.t[:, 0:QCW], self.ones_r[0:1, :], SN.mrow[0:1, :], start=True, stop=True))
            bB1.mm_tok = sig("tensor", lambda te: te.matmul(
                bB1.t[:, 0:QCW], self.ones_r[0:1, :], SN.rrow[0:1, :], start=True, stop=True))
            tk["rows_free"] = bB1.mm_tok
            see("vector", bB1.mm_tok)
            see("vector", tk.get("xnf_free"))
            tmps = [SN.tmp512, SN.tmp512b]
            sub_toks = [None, None]
            mul_toks = [None, None]
            for d in range(DCH):
                tb = tmps[d % 2]
                see("vector", mul_toks[d % 2])
                sub_toks[d % 2] = sig("vector", lambda v, d=d, xcp=xcp, tb=tb: v.tensor_sub(
                    tb[:, :], xcp[:, d * QCW:(d + 1) * QCW], bB0.t[:, 0:QCW]))
                see("vector", sub_toks[d % 2])
                mul_toks[d % 2] = sig("vector", lambda v, d=d, tb=tb: v.tensor_mul(
                    SN.xnf[:, d * QCW:(d + 1) * QCW], tb[:, :], bB1.t[:, 0:QCW]))
            tok_xnf = mul_toks[(DCH - 1) % 2]
            bB0.free_tok = bB1.free_tok = tok_xnf
            tk[("xc_free", par)] = tok_xnf
            see("scalar", tok_xnf)
            see("scalar", tk.get("xnb_free"))
            W("scalar", self.p_xn_st)
            tok_xnb = sig("scalar", lambda se: se.activation(SN.xnb[:, :], SN.xnf[:, :], AF.Copy))
            see("sync", tok_xnb)
            self.dma("sync", lambda sy, s=s, tc=tc: sy.dma_start(
                out=self.xn_scr[s, tc], in_=SN.xnb[:, :]), self.p_xn_st)
            if os.environ.get("SKIP_ROUTER") == "1":
                tk["xnb_free"] = None
                tk["xnf_free"] = None
                continue
            see("tensor", tok_xnf)
            see("tensor", bM.free_tok)
            for d in range(DCH):
                fn = lambda te, d=d: te.matmul(
                    bM.t[0:Elay, 0:QCW], gatef[:, d * Elay:(d + 1) * Elay],
                    SN.xnf[:, d * QCW:(d + 1) * QCW], start=(d == 0), stop=(d == DCH - 1))
                if d == DCH - 1:
                    bM.mm_tok = sig("tensor", fn)
                else:
                    E("tensor", fn)
            tk["xnf_free"] = bM.mm_tok
            tk["xnb_free"] = tok_xnb
            see("scalar", bM.mm_tok)
            see("scalar", tk.get("sct_free"))
            tok_sg = sig("scalar", lambda se: se.activation(
                SN.sct[0:Elay, :], bM.t[0:Elay, 0:QCW], AF.Exp,
                bias=nlb[0:Elay, 0:1], scale=-1.0))
            bM.free_tok = tok_sg
            see("vector", tok_sg)
            see("vector", tk.get("sct2_free"))
            tok_s1 = sig("vector", lambda v: v.tensor_scalar_add(SN.sct2[0:Elay, :],
                                                                 SN.sct[0:Elay, :], 1.0))
            see("vector", tok_s1)
            tok_sct = sig("vector", lambda v: v.reciprocal(SN.sct2[0:Elay, :], SN.sct2[0:Elay, :]))
            tk["sct_free"] = tok_s1
            see("tensor", tok_sct)
            see("tensor", bR.free_tok)
            for w in range(4):
                fn = lambda te, w=w: te.transpose(bR.t[:, w * 16:w * 16 + Elay],
                                                  SN.sct2[0:Elay, w * 128:(w + 1) * 128],
                                                  self.eye_f[0:Elay, 0:Elay])
                if w == 3:
                    bR.mm_tok = sig("tensor", fn)
                else:
                    E("tensor", fn)
            tk["sct2_free"] = bR.mm_tok
            see("vector", bR.mm_tok)
            see("vector", tk.get("t2_free"))
            def step(name, fn_w):
                tok = None
                for w in range(4):
                    tok = sig("vector", lambda v, w=w: fn_w(v, w))
                return tok
            co = lambda w: slice(w * 16, w * 16 + Elay)
            c1 = lambda w: slice(w, w + 1)
            tok_cp = step("cp", lambda v, w: v.tensor_copy(SN.sctok4[:, co(w)], bR.t[:, w * 16:w * 16 + Elay]))
            see("vector", tok_cp)
            tok_m1 = step("max1", lambda v, w: v.reduce_max(t2["max1"][:, c1(w)], SN.sctok4[:, co(w)], AX.X))
            see("vector", tok_m1)
            tok_ms1 = step("m1", lambda v, w: v.tensor_single_scalar(
                t2["m1"][:, co(w)], SN.sctok4[:, co(w)], t2["max1"][:, c1(w)], ALU.is_ge))
            see("vector", tok_ms1)
            tok_s2 = step("s2", lambda v, w: v.scalar_tensor_tensor(
                t2["s2"][:, co(w)], t2["m1"][:, co(w)], NEG_BIG, SN.sctok4[:, co(w)],
                ALU.mult, ALU.add))
            see("vector", tok_s2)
            tok_m2 = step("max2", lambda v, w: v.reduce_max(t2["max2"][:, c1(w)], t2["s2"][:, co(w)], AX.X))
            see("vector", tok_m2)
            tok_ms2 = step("m2", lambda v, w: v.tensor_single_scalar(
                t2["m2"][:, co(w)], t2["s2"][:, co(w)], t2["max2"][:, c1(w)], ALU.is_ge))
            see("vector", tok_ms2)
            tok_dn = step("den", lambda v, w: v.tensor_add(t2["den"][:, c1(w)], t2["max1"][:, c1(w)],
                                                           t2["max2"][:, c1(w)]))
            if layer == 0:
                see("vector", tok_dn)
                tok_dn = step("dne", lambda v, w: v.tensor_scalar_add(t2["den"][:, c1(w)],
                                                                     t2["den"][:, c1(w)], 1e-8))
            see("vector", tok_dn)
            tok_rd = step("rden", lambda v, w: v.reciprocal(t2["rden"][:, c1(w)], t2["den"][:, c1(w)]))
            see("vector", tok_rd)
            tok_a1 = step("a1", lambda v, w: v.tensor_mul(t2["a1"][:, c1(w)], t2["max1"][:, c1(w)],
                                                          t2["rden"][:, c1(w)]))
            see("vector", tok_a1)
            tok_a2 = step("a2", lambda v, w: v.tensor_mul(t2["a2"][:, c1(w)], t2["max2"][:, c1(w)],
                                                          t2["rden"][:, c1(w)]))
            see("vector", tok_a2)
            tok_w1 = step("w1", lambda v, w: v.tensor_scalar_mul(t2["w1"][:, co(w)], t2["m1"][:, co(w)],
                                                                 t2["a1"][:, c1(w)]))
            see("vector", tok_w1)
            tok_w2 = step("w2", lambda v, w: v.tensor_scalar_mul(t2["w2"][:, co(w)], t2["m2"][:, co(w)],
                                                                 t2["a2"][:, c1(w)]))
            see("vector", tok_w2)
            tok_wt = step("wtok", lambda v, w: v.tensor_add(t2["wtok"][:, co(w)], t2["w1"][:, co(w)],
                                                            t2["w2"][:, co(w)]))
            bR.free_tok = tok_cp
            if layer == 0:
                see("vector", tok_ms2)
                tok_msel = None
                for w in range(4):
                    tile = tc * 4 + w
                    tok_msel = sig("vector", lambda v, s=s, tile=tile, w=w: v.tensor_add(
                        self.msel_st[s][:, tile * 8:tile * 8 + 8],
                        t2["m1"][:, w * 16:w * 16 + 8], t2["m2"][:, w * 16:w * 16 + 8]))
                see("tensor", tok_wt)
                see("tensor", tok_msel)
                tok_st = None
                for w in range(4):
                    tile = tc * 4 + w
                    first = (s == 0 and tile == 0)
                    last = (s == S - 1 and tile == 15)
                    E("tensor", lambda te, w=w, first=first: te.matmul(
                        bT.t[0:8, 0:1], t2["wtok"][:, w * 16:w * 16 + 8], self.ones_f[:, 0:1],
                        start=first, stop=False))
                    tok_st = sig("tensor", lambda te, s=s, tile=tile, last=last: te.matmul(
                        bT.t[0:8, 1:2], self.msel_st[s][:, tile * 8:tile * 8 + 8],
                        self.ones_f[:, 0:1], start=False, stop=last))
                    if last:
                        bT.mm_tok = tok_st
                tk["t2_free"] = tok_st
            else:
                see("tensor", tok_wt)
                tok_tr = None
                for w in range(4):
                    tile = tc * 4 + w
                    see("tensor", bW_.free_tok)
                    bW_.mm_tok = sig("tensor", lambda te, w=w: te.transpose(
                        bW_.t[0:16, 0:128], t2["wtok"][:, w * 16:w * 16 + 16], self.eye_f[:, :]))
                    tok_tr = bW_.mm_tok
                    see("vector", bW_.mm_tok)
                    bW_.free_tok = sig("vector", lambda v, tile=tile: v.tensor_copy(
                        self.w1T[s][0:16, tile * 128:(tile + 1) * 128], bW_.t[0:16, 0:128]))
                tk["t2_free"] = tok_tr

    def collective_wmean(self):
        E, sig, see, W = self.E, self.sig, self.see, self.W
        bT = self.banks[6]
        see("vector", bT.mm_tok)
        tok_ev = sig("vector", lambda v: v.tensor_copy(self.stats_loc[0:8, 0:2], bT.t[0:8, 0:2]))
        bT.free_tok = tok_ev
        see("gpsimd", tok_ev)
        self.dma("gpsimd", lambda g: g.dma_start(out=self.cc_in.ap(), in_=self.stats_loc[:, :]),
                 self.p_cc)
        W("gpsimd", self.p_cc)
        self.p_ccc.count += 1
        self.q["gpsimd"].append(("op", lambda g: g.collective_compute(
            "AllReduce", ALU.add, replica_groups=[list(range(NCORES))],
            ins=[self.cc_in.ap().opt()], outs=[self.cc_out.ap().opt()]), self.p_ccc, 1))
        W("gpsimd", self.p_ccc)
        self.dma("gpsimd", lambda g: g.dma_start(out=self.stats_gl[:, :],
                                                 in_=self.cc_out.ap()[0:8, 0:2]), self.p_cc)
        W("vector", self.p_cc)
        tok_1 = sig("vector", lambda v: v.tensor_scalar_max(self.wm_c[:, :], self.stats_gl[0:8, 1:2], 1.0))
        see("vector", tok_1)
        tok_2 = sig("vector", lambda v: v.reciprocal(self.wm_rc[:, :], self.wm_c[:, :]))
        see("vector", tok_2)
        tok_wm = sig("vector", lambda v: v.tensor_mul(self.wmean[:, :], self.stats_gl[0:8, 0:1],
                                                      self.wm_rc[:, :]))
        bW_ = self.banks[7]
        see("tensor", tok_wm)
        for s in range(S):
            for tile in range(16):
                see("tensor", bW_.free_tok)
                bW_.mm_tok = sig("tensor", lambda te, s=s, tile=tile: te.transpose(
                    bW_.t[0:8, 0:128], self.msel_st[s][:, tile * 8:tile * 8 + 8],
                    self.eye_f[:, :]))
                see("vector", bW_.mm_tok)
                see("vector", tok_wm)
                bW_.free_tok = sig("vector", lambda v, s=s, tile=tile: v.tensor_scalar_mul(
                    self.w0T[s][0:8, tile * 128:(tile + 1) * 128], bW_.t[0:8, 0:128],
                    self.wmean[0:8, 0:1]))

    def load_wslot(self, slot, dram_ap, cols_layout):
        see, dma = self.see, self.dma
        see("gpsimd", self.wslot_free[slot])
        if len(cols_layout) == 2:
            c, n = cols_layout
            out = self.wslot[slot][:, 0:c * n].rearrange("p (c n) -> p c n", c=c)
            in_ = dram_ap.transpose([1, 0, 2])
        else:
            g, c, n = cols_layout
            out = self.wslot[slot][:, 0:g * c * n].rearrange("p (g c n) -> p g c n", g=g, c=c)
            in_ = dram_ap.transpose([2, 0, 1, 3])
        tok = dma("gpsimd", lambda gp, out=out, in_=in_: gp.dma_start(out=out, in_=in_), self.p_w)
        self.wslot_tok[slot] = tok
        return tok

    def load_whalf(self, slot, half, dram_ap, tok_free):
        see, dma = self.see, self.dma
        see("gpsimd", tok_free)
        out = self.wslot[slot][:, half * 4096:(half + 1) * 4096].rearrange(
            "p (c n) -> p c n", c=4)
        in_ = dram_ap.transpose([1, 0, 2])
        return dma("gpsimd", lambda gp, out=out, in_=in_: gp.dma_start(out=out, in_=in_), self.p_w)

    def rope(self, src_bf, dst_ap, win, tok_src, tok_dst_free):
        SN = self.snap("rc1 rc2")
        """dst = src*cos + swap32(src)*ssin over a 512-token window.
        Returns token of the final add (frees src and rc scratch)."""
        sig, see = self.sig, self.see
        see("vector", tok_src)
        see("vector", self.tok.get("rope_fr"))
        see("vector", tok_dst_free)
        t1 = sig("vector", lambda v: v.tensor_mul(SN.rc1[:, :], src_bf[:, :],
                                                  self.cos[:, win]))
        toks2 = []
        for r0, r1 in ((0, 32), (32, 0), (64, 96), (96, 64)):
            toks2.append(sig("vector", lambda v, r0=r0, r1=r1: v.tensor_mul(
                SN.rc2[r0:r0 + 32, :], src_bf[r1:r1 + 32, :], self.ssin2[r1:r1 + 32, win])))
        see("vector", t1)
        see("vector", toks2[-1])
        t3 = sig("vector", lambda v, dst_ap=dst_ap: v.tensor_add(dst_ap, SN.rc1[:, :],
                                                                 SN.rc2[:, :]))
        self.tok["rope_fr"] = t3
        return t3

    def phase_att(self, L, s):
        SN = self.snap("KT Vp QTc och xn_ch k_bf q_bf exp2 oraw lrow lrinv linv_b bvrow bvb maskb")
        E_, sig, see, W, dma = self.E, self.sig, self.see, self.W, self.dma
        i = self.i
        pre = "l0_" if L == 0 else "l1_"
        wq_nm, wk_nm, wv_nm = pre + "wq", pre + "wk", pre + "wv"
        wo_nm = "l0_wo" if L == 0 else "l1_wp"
        bq = self.bias_s[pre + "bq"]
        bk = self.bias_s[pre + "bk"]
        bo = self.bias_s["l0_bo" if L == 0 else "l1_bp"]
        tk = self.tok
        b01 = [self.abank[0], self.abank[1]]
        bav = [self.abank[2], self.abank[3]]
        bsc2 = self.psc2
        b6, b7 = self.abank[2], self.abank[3]
        blr = self.abank[0]

        self.load_wslot(0, i[wk_nm], (DCH, D))
        self.load_wslot(1, i[wv_nm], (DCH, D))
        self.load_wslot(2, i[wq_nm], (DCH, D))
        tok_bvr = dma("gpsimd", lambda gp: gp.dma_start(out=SN.bvrow[:, :], in_=i[pre + "bv"]),
                      self.p_w)
        for half in range(2):
            see("tensor", tok_bvr)
            see("tensor", b6.free_tok)
            b6.mm_tok = sig("tensor", lambda te, half=half: te.matmul(
                b6.t[:, 0:QCW], self.ones_rb[0:1, :], SN.bvrow[0:1, half * QCW:(half + 1) * QCW],
                start=True, stop=True))
            see("scalar", b6.mm_tok)
            b6.free_tok = sig("scalar", lambda se, half=half: se.activation(
                SN.bvb[:, half * QCW:(half + 1) * QCW], b6.t[:, 0:QCW], AF.Copy))
        tok_bvb = b6.free_tok
        tok_vp1 = sig("vector", lambda v: v.memset(SN.Vp[:, :], 1.0))
        for tc in range(NQC):
            win = slice(tc * QCW, (tc + 1) * QCW)
            par = tc % 2
            xch = SN.xn_ch[par]
            see("sync", tk.get(("xnch_fr", par)))
            self.dma("sync", lambda sy, s=s, tc=tc, xch=xch: sy.dma_start(
                out=xch[:, :], in_=self.xn_scr[s, tc]), self.p_xn_ld)
            W("tensor", self.p_xn_ld)
            W("tensor", self.p_w, self.wslot_tok[0][1])
            for j in range(DCH):
                bk_ = b01[j % 2]
                see("tensor", bk_.free_tok)
                for d in range(DCH):
                    fn = lambda te, j=j, d=d, bk_=bk_, xch=xch: te.matmul(
                        bk_.t[:, 0:QCW], self.wslot[0][:, d * D + j * 128:d * D + j * 128 + 128],
                        xch[:, d * QCW:(d + 1) * QCW],
                        start=(d == 0), stop=(d == DCH - 1))
                    if d == DCH - 1:
                        bk_.mm_tok = sig("tensor", fn)
                    else:
                        E_("tensor", fn)
                if tc == NQC - 1 and j == DCH - 1:
                    self.wslot_free[0] = bk_.mm_tok
                see("scalar", bk_.mm_tok)
                see("scalar", tk.get("kbf_fr"))
                tok_ke = sig("scalar", lambda se, j=j, bk_=bk_: se.activation(
                    SN.k_bf[:, :], bk_.t[:, 0:QCW], AF.Identity, bias=bk[:, j:j + 1]))
                bk_.free_tok = tok_ke
                dst = SN.KT[:, j * T + tc * QCW:j * T + (tc + 1) * QCW]
                t3 = self.rope(SN.k_bf, dst, win, tok_ke, None)
                tk["kbf_fr"] = t3
            W("tensor", self.p_w, self.wslot_tok[1][1])
            for tt in range(4):
                ktile = tc * 4 + tt
                toks_h = []
                for half in range(2):
                    bk_ = [b6, b7][half]
                    see("tensor", bk_.free_tok)
                    for d in range(DCH):
                        fn = lambda te, tt=tt, half=half, d=d, bk_=bk_, xch=xch: te.matmul(
                            bk_.t[:, 0:QCW],
                            xch[:, d * QCW + tt * 128:d * QCW + tt * 128 + 128],
                            self.wslot[1][:, d * D + half * QCW:d * D + (half + 1) * QCW],
                            start=(d == 0), stop=(d == DCH - 1))
                        if d == DCH - 1:
                            bk_.mm_tok = sig("tensor", fn)
                        else:
                            E_("tensor", fn)
                    see("vector", bk_.mm_tok)
                    see("vector", tok_bvb)
                    see("vector", tok_vp1)
                    base = ktile * (H * 65) + (8 * half) * 65
                    vout = SN.Vp[:, base:base + 8 * 65].rearrange("p (h c) -> p h c", c=65)[:, :, 0:64]
                    tok_v = sig("vector", lambda v, half=half, bk_=bk_, vout=vout: v.tensor_add(
                        vout, bk_.t[:, 0:QCW].rearrange("p (h c) -> p h c", c=64),
                        SN.bvb[:, half * QCW:(half + 1) * QCW].rearrange("p (h c) -> p h c", c=64)))
                    bk_.free_tok = tok_v
                    toks_h.append(tok_v)
                tok_vp = toks_h[-1]
                tk["vbf_fr"] = tok_vp
            tk[("xnch_fr", par)] = bk_.mm_tok
            if tc == NQC - 1:
                self.wslot_free[1] = tk["vbf_fr"]
        self.load_wslot(1, i[wo_nm], (DCH, D))

        tok_last_kt = tk.get("vbf_fr")
        for qc in range(NQC):
            win = slice(qc * QCW, (qc + 1) * QCW)
            par = qc % 2
            xch = SN.xn_ch[par]
            see("sync", tk.get(("xnch_fr", par)))
            self.dma("sync", lambda sy, s=s, qc=qc, xch=xch: sy.dma_start(
                out=xch[:, :], in_=self.xn_scr[s, qc]), self.p_xn_ld)
            W("tensor", self.p_xn_ld)
            W("tensor", self.p_w, self.wslot_tok[2][1])
            W("vector", self.p_ao)
            for j in range(DCH):
                bk_ = b01[j % 2]
                see("tensor", bk_.free_tok)
                see("tensor", tk.get("qtc_fr"))
                for d in range(DCH):
                    fn = lambda te, j=j, d=d, bk_=bk_, xch=xch: te.matmul(
                        bk_.t[:, 0:QCW], self.wslot[2][:, d * D + j * 128:d * D + j * 128 + 128],
                        xch[:, d * QCW:(d + 1) * QCW],
                        start=(d == 0), stop=(d == DCH - 1))
                    if d == DCH - 1:
                        bk_.mm_tok = sig("tensor", fn)
                    else:
                        E_("tensor", fn)
                if qc == NQC - 1 and j == DCH - 1:
                    self.wslot_free[2] = bk_.mm_tok
                see("vector", bk_.mm_tok)
                see("vector", tk.get("qbf_fr"))
                tok_qe = sig("vector", lambda v, j=j, bk_=bk_: v.tensor_scalar_add(
                    SN.q_bf[:, :], bk_.t[:, 0:QCW], bq[:, j:j + 1]))
                bk_.free_tok = tok_qe
                dst = SN.QTc[:, j * QCW:(j + 1) * QCW]
                t3 = self.rope(SN.q_bf, dst, win, tok_qe, tk.get("qtc_fr"))
                tk["qbf_fr"] = t3
            tok_q_ready = tk["qbf_fr"]
            tk[("xnch_fr", par)] = bk_.mm_tok
            tok_wo_prev = tk.get("och_fr")
            for hp in range(H // 2):
                nkt = NKT if L == 0 else 4 * qc + 4
                bvA = bav[0]
                bvB = bav[1]
                see("tensor", tok_q_ready)
                see("tensor", bvA.free_tok)
                see("tensor", bvB.free_tok)
                def sc_kt(kt):
                    bsp = bsc2[kt % 2]
                    see("tensor", bsp.free_tok)
                    for half in range(2):
                        fn = lambda te, kt=kt, half=half, bsp=bsp, hp=hp: te.matmul(
                            bsp.t[:, half * QCW:(half + 1) * QCW],
                            SN.KT[64 * half:64 * half + 64, hp * T + kt * 128:hp * T + kt * 128 + 128],
                            SN.QTc[64 * half:64 * half + 64, hp * QCW:(hp + 1) * QCW],
                            start=True, stop=True)
                        if half == 1:
                            bsp.mm_tok = sig("tensor", fn)
                        else:
                            E_("tensor", fn)
                def exp_kt(kt):
                    p = kt % 2
                    bsp = bsc2[p]
                    see("scalar", bsp.mm_tok)
                    see("scalar", tk.get(("exp_fr", p)))
                    tok_exp = sig("scalar", lambda se, p=p, bsp=bsp: se.activation(
                        SN.exp2[p][:, :], bsp.t[:, :], AF.Exp, scale=0.125))
                    bsp.free_tok = tok_exp
                    if L == 1 and kt >= 4 * qc:
                        m = kt - 4 * qc
                        see("vector", tok_exp)
                        tok_exp = sig("vector", lambda v, p=p, m=m: v.tensor_mul(
                            SN.exp2[p][:, :], SN.exp2[p][:, :],
                            SN.maskb[:, m * 1024:(m + 1) * 1024]))
                    return tok_exp
                def av_kt(kt, tok_exp):
                    p = kt % 2
                    see("tensor", tok_exp)
                    tok_av = None
                    for half in range(2):
                        h = 2 * hp + half
                        bv_ = (bvA, bvB)[half]
                        fn = lambda te, kt=kt, p=p, h=h, half=half, bv_=bv_, nkt=nkt: te.matmul(
                            bv_.t[0:65, 0:QCW],
                            SN.Vp[:, kt * (H * 65) + h * 65:kt * (H * 65) + h * 65 + 65],
                            SN.exp2[p][:, half * QCW:(half + 1) * QCW],
                            start=(kt == 0), stop=(kt == nkt - 1))
                        if half == 1:
                            tok_av = sig("tensor", fn)
                        else:
                            E_("tensor", fn)
                    tk[("exp_fr", p)] = tok_av
                    return tok_av
                sc_kt(0)
                etoks = {0: exp_kt(0)}
                for kt in range(nkt):
                    if kt + 1 < nkt:
                        sc_kt(kt + 1)
                        etoks[kt + 1] = exp_kt(kt + 1)
                    tok_av = av_kt(kt, etoks.pop(kt))
                bvA.mm_tok = bvB.mm_tok = tok_av
                pp2 = hp % 2
                for half in range(2):
                    bv_ = (bvA, bvB)[half]
                    see("vector", tok_av)
                    see("vector", tk.get(("oraw_fr", pp2, half)))
                    t_ev = sig("vector", lambda v, half=half, pp2=pp2, bv_=bv_: v.tensor_copy(
                        SN.oraw[0:65, (2 * pp2 + half) * QCW:(2 * pp2 + half + 1) * QCW],
                        bv_.t[0:65, 0:QCW]))
                    bv_.free_tok = t_ev
                    tk[("oraw_rdy", pp2, half)] = t_ev

                def norm_emit(hp2):
                    pp3 = hp2 % 2
                    for half in range(2):
                        h = 2 * hp2 + half
                        col = (2 * pp3 + half) * QCW
                        see("scalar", tk.get(("oraw_rdy", pp3, half)))
                        see("scalar", tk.get(("lrow_fr", half)))
                        t_l = sig("scalar", lambda se, half=half, col=col: se.activation(
                            SN.lrow[half][:, :], SN.oraw[64:65, col:col + QCW], AF.Ln))
                        see("scalar", t_l)
                        t_r = sig("scalar", lambda se, half=half: se.activation(
                            SN.lrinv[half][:, :], SN.lrow[half][:, :], AF.Exp, scale=-1.0))
                        tk[("lrow_fr", half)] = t_r
                        blr_ = b01[half]
                        see("tensor", t_r)
                        see("tensor", blr_.free_tok)
                        blr_.mm_tok = sig("tensor", lambda te, half=half, blr_=blr_: te.matmul(
                            blr_.t[0:64, 0:QCW], self.ones_rb[0:1, 0:64],
                            SN.lrinv[half][0:1, :], start=True, stop=True))
                        see("vector", blr_.mm_tok)
                        see("vector", tk.get(("linvb_fr", half)))
                        tok_lb = sig("vector", lambda v, half=half, blr_=blr_: v.tensor_copy(
                            SN.linv_b[:, half * QCW:(half + 1) * QCW],
                            blr_.t[0:64, 0:QCW]))
                        blr_.free_tok = tok_lb
                        see("vector", tok_lb)
                        see("vector", tok_wo_prev)
                        tok_on = sig("vector", lambda v, half=half, hp2=hp2, col=col: v.tensor_mul(
                            SN.och[64 * half:64 * half + 64, hp2 * QCW:(hp2 + 1) * QCW],
                            SN.oraw[0:64, col:col + QCW],
                            SN.linv_b[0:64, half * QCW:(half + 1) * QCW]))
                        tk[("linvb_fr", half)] = tok_on
                        tk[("oraw_fr", pp3, half)] = tok_on
                        tk[("och", h)] = tok_on
                if hp > 0:
                    norm_emit(hp - 1)
                if hp == H // 2 - 1:
                    norm_emit(hp)
            W("tensor", self.p_w, self.wslot_tok[1][1])
            for h in range(H):
                see("tensor", tk[("och", h)])
            tok_wo_last = None
            tok_evac_last = None
            for i2 in range(DCH):
                bk_ = b01[i2 % 2]
                see("tensor", bk_.free_tok)
                for j in range(DCH):
                    fn = lambda te, i2=i2, j=j, bk_=bk_: te.matmul(
                        bk_.t[:, 0:QCW], self.wslot[1][:, j * D + i2 * 128:j * D + i2 * 128 + 128],
                        SN.och[:, j * QCW:(j + 1) * QCW],
                        start=(j == 0), stop=(j == DCH - 1))
                    if j == DCH - 1:
                        bk_.mm_tok = sig("tensor", fn)
                    else:
                        E_("tensor", fn)
                tok_wo_last = bk_.mm_tok
                see("vector", bk_.mm_tok)
                W("vector", self.p_ao)
                tok_ev = sig("vector", lambda v, i2=i2, bk_=bk_: v.tensor_scalar_add(
                    SN.QTc[:, i2 * QCW:(i2 + 1) * QCW], bk_.t[:, 0:QCW],
                    bo[:, i2:i2 + 1]))
                bk_.free_tok = tok_ev
                tok_evac_last = tok_ev
            tk["och_fr"] = tok_wo_last
            tk["qtc_fr"] = tok_wo_last
            see("sync", tok_evac_last)
            dma("sync", lambda sy, qc=qc: sy.dma_start(out=self.ao_scr[qc], in_=SN.QTc[:, :]),
                self.p_ao)
        self.wslot_free[1] = tk["och_fr"]

    def phase_ffn(self, L, s):
        SN = self.snap("xn_s acc att_tk xres_tk m_buf wbc_sb sg_sb su_sb selb")
        E_, sig, see, W, dma = self.E, self.sig, self.see, self.W, self.dma
        i = self.i
        tk = self.tok
        Elay = 8 if L == 0 else 16
        wT = self.w0T[s] if L == 0 else self.w1T[s]
        gu_nm = "e0_gu" if L == 0 else "e1_gu"
        d_nm = "e0_d" if L == 0 else "e1_d"
        pg_b = [self.fbank[0], self.fbank[2]]
        pu_b = [self.fbank[1], self.fbank[3]]
        dacc = self.fbank[4:8]

        for tc in range(NQC):
            out = SN.xn_s[:, :].rearrange("p (c t) -> p c t", c=DCH)[:, :, tc * QCW:(tc + 1) * QCW]
            in_ = self.xn_scr[s, tc].rearrange("p (c t) -> p c t", c=DCH)
            dma("sync", lambda sy, out=out, in_=in_: sy.dma_start(out=out, in_=in_), self.p_xn_ld)
        W("tensor", self.p_xn_ld)

        groups = [[e, e + 1] for e in range(0, Elay, 2)] + [["sh"]]
        acc_tok = [None] * DCH
        tok_m_fr = [None, None]
        for gi, group in enumerate(groups):
            shared = group[0] == "sh"
            itn = 8 if (shared and L == 0) else 4
            if shared:
                if L == 0:
                    tok_g = self.load_wslot(0, i["s0_g"], (DCH, D))
                    tok_u = self.load_wslot(1, i["s0_u"], (DCH, D))
                    see("gpsimd", self.wslot_free[2])
                    tok_d0 = self.load_wslot(2, i["s0_d"], (DCH, D))
                    gu_toks = [(tok_g, tok_u)]
                    d_toks = [tok_d0]
                else:
                    gu_toks = [(self.load_wslot(0, i["s1_gu"], (2, DCH, I_EXP)),) * 2]
                    d_toks = [self.load_whalf(2, 0, i["s1_d"], self.wslot_free[2])]
            else:
                gu_toks = []
                d_toks = []
                for k, e in enumerate(group):
                    t1 = self.load_wslot(k, i[gu_nm][e], (2, DCH, I_EXP))
                    gu_toks.append((t1, t1))
                    d_toks.append(self.load_whalf(2, k, i[d_nm][e],
                                                  self.wslot_free[2] if k == 0 else None))
            FW = 2 * TKW
            for tkc in range(NQC):
                if not shared:
                    for k, e in enumerate(group):
                        bb = pg_b[k % 2]
                        see("tensor", bb.free_tok)
                        bb.mm_tok = sig("tensor", lambda te, e=e, tkc=tkc, bb=bb: te.matmul(
                            bb.t[:, 0:FW], SN.selb[0:Elay, e * 128:(e + 1) * 128],
                            wT[0:Elay, tkc * FW:(tkc + 1) * FW], start=True, stop=True))
                        see("scalar", bb.mm_tok)
                        see("scalar", tk.get("wu_fr"))
                        tok_w = sig("scalar", lambda se, k=k, bb=bb: se.activation(
                            SN.wbc_sb[:, k * FW:(k + 1) * FW],
                            bb.t[:, 0:FW], AF.Copy))
                        bb.free_tok = tok_w
                        tk[("wbc", k)] = tok_w
                def m_ap(k, it):
                    if shared and L == 0:
                        return SN.m_buf[it // 4][:, (it % 4) * FW:(it % 4) * FW + FW]
                    return SN.m_buf[k][:, it * FW:(it + 1) * FW]
                tok_m_done = [None, None]
                last_f1 = None
                for k, u in enumerate(group):
                    for it in range(itn):
                        pg = pg_b[it % 2]
                        pu = pu_b[it % 2]
                        see("tensor", pg.free_tok)
                        see("tensor", pu.free_tok)
                        if tkc == 0 and it == 0:
                            see("tensor", gu_toks[k][0] if not (shared and L == 0) else gu_toks[0][0])
                            if shared and L == 0:
                                see("tensor", gu_toks[0][1])
                        for d in range(DCH):
                            if shared and L == 0:
                                g_ap = lambda d=d, it=it: self.wslot[0][:, d * D + it * 128:d * D + it * 128 + 128]
                                u_ap = lambda d=d, it=it: self.wslot[1][:, d * D + it * 128:d * D + it * 128 + 128]
                            else:
                                g_ap = lambda d=d, it=it, k=k: self.wslot[k][:, d * I_EXP + it * 128:d * I_EXP + it * 128 + 128]
                                u_ap = lambda d=d, it=it, k=k: self.wslot[k][:, 4096 + d * I_EXP + it * 128:4096 + d * I_EXP + it * 128 + 128]
                            rhs = lambda d=d, tkc=tkc: SN.xn_s[:, d * T + tkc * FW:d * T + (tkc + 1) * FW]
                            fg = lambda te, g_ap=g_ap, rhs=rhs, d=d, pg=pg: te.matmul(
                                pg.t[:, 0:FW], g_ap(), rhs(), start=(d == 0), stop=(d == DCH - 1))
                            fu = lambda te, u_ap=u_ap, rhs=rhs, d=d, pu=pu: te.matmul(
                                pu.t[:, 0:FW], u_ap(), rhs(), start=(d == 0), stop=(d == DCH - 1))
                            if d == DCH - 1:
                                pg.mm_tok = sig("tensor", fg)
                                pu.mm_tok = sig("tensor", fu)
                                last_f1 = pu.mm_tok
                            else:
                                E_("tensor", fg)
                                E_("tensor", fu)
                        see("scalar", pg.mm_tok)
                        see("scalar", tk.get("sg_fr"))
                        tok_sg = sig("scalar", lambda se, pg=pg: se.activation(
                            SN.sg_sb[:, :], pg.t[:, 0:FW], AF.Silu))
                        pg.free_tok = tok_sg
                        see("vector", pu.mm_tok)
                        see("vector", tk.get("su_fr"))
                        if not shared:
                            see("vector", tk.get(("wbc", k)))
                            tok_su = sig("vector", lambda v, pu=pu, k=k: v.tensor_mul(
                                SN.su_sb[:, :], pu.t[:, 0:FW],
                                SN.wbc_sb[:, k * FW:(k + 1) * FW]))
                        else:
                            tok_su = sig("vector", lambda v, pu=pu: v.tensor_copy(
                                SN.su_sb[:, :], pu.t[:, 0:FW]))
                        pu.free_tok = tok_su
                        see("vector", tok_sg)
                        see("vector", tok_su)
                        see("vector", tok_m_fr[k if not (shared and L == 0) else it // 4])
                        tok_m = sig("vector", lambda v, ap=m_ap(k, it): v.tensor_mul(
                            ap, SN.sg_sb[:, :], SN.su_sb[:, :]))
                        tk["sg_fr"] = tok_m
                        tk["su_fr"] = tok_m
                        tk["wu_fr"] = tok_m
                        tok_m_done[k] = tok_m
                tok_m_fr = tok_m_done
                for k in range(len(group)):
                    see("tensor", tok_m_done[k])
                if tkc == 0:
                    for td in d_toks:
                        see("tensor", td)
                tok_f2 = [None] * DCH
                for hp in range(2):
                    for i2 in range(hp * 4, hp * 4 + 4):
                        bb = dacc[i2 % 4]
                        see("tensor", acc_tok[i2])
                        nmm = len(group) * itn
                        c = 0
                        for k in range(len(group)):
                            for it in range(itn):
                                if shared and L == 0:
                                    d_ap = lambda it=it, i2=i2: self.wslot[2][:, it * D + i2 * 128:it * D + i2 * 128 + 128]
                                else:
                                    d_ap = lambda it=it, i2=i2, k=k: self.wslot[2][:, k * 4096 + it * D + i2 * 128:k * 4096 + it * D + i2 * 128 + 128]
                                fn = lambda te, d_ap=d_ap, m=m_ap(k, it), bb=bb, c=c, nmm=nmm: te.matmul(
                                    bb.t[:, 0:FW], d_ap(), m,
                                    start=(c == 0), stop=(c == nmm - 1))
                                if c == nmm - 1:
                                    tok_f2[i2] = sig("tensor", fn)
                                else:
                                    E_("tensor", fn)
                                c += 1
                    for i2 in range(hp * 4, hp * 4 + 4):
                        bb = dacc[i2 % 4]
                        see("vector", tok_f2[i2])
                        see("vector", acc_tok[i2])
                        aslice = SN.acc[:, i2 * T + tkc * FW:i2 * T + (tkc + 1) * FW]
                        if gi == 0:
                            acc_tok[i2] = sig("vector", lambda v, aslice=aslice, bb=bb: v.tensor_copy(
                                aslice, bb.t[:, 0:FW]))
                        else:
                            acc_tok[i2] = sig("vector", lambda v, aslice=aslice, bb=bb: v.tensor_add(
                                aslice, aslice, bb.t[:, 0:FW]))
            self.wslot_free[0] = last_f1
            self.wslot_free[1] = last_f1
            self.wslot_free[2] = tok_f2[DCH - 1]

        tok_cmb_prev = None
        tok_out_prev = [None, None]
        for tkc in range(NTK):
            par = 0
            tc = tkc // 2
            half = tkc % 2
            see("sync", tok_cmb_prev)
            if L == 0:
                src = self.i["xT"][s].rearrange("(c p) t -> p c t", p=128)[:, :, tkc * TKW:(tkc + 1) * TKW]
            else:
                src = self.x1_scr[s, tkc].rearrange("p (c u) -> p c u", c=DCH)
            t_res = dma("sync", lambda sy, src=src, par=par: sy.dma_start(
                out=SN.xres_tk[par][:, :].rearrange("p (c t) -> p c t", c=DCH), in_=src),
                self.p_res)
            aoin = self.ao_scr[tc].rearrange("p (i q) -> p i q", i=DCH)[:, :, half * TKW:(half + 1) * TKW]
            t_ao = dma("sync", lambda sy, aoin=aoin, par=par: sy.dma_start(
                out=SN.att_tk[par][:, :].rearrange("p (c t) -> p c t", c=DCH), in_=aoin),
                self.p_ao)
            accv = SN.acc[:, :].rearrange("p (c t) -> p c t", c=DCH)[:, :, tkc * TKW:(tkc + 1) * TKW]
            attv = SN.att_tk[par][:, :].rearrange("p (c t) -> p c t", c=DCH)
            resv = SN.xres_tk[par][:, :].rearrange("p (c t) -> p c t", c=DCH)
            xnv = SN.xn_s[:, :].rearrange("p (c t) -> p c t", c=DCH)[:, :, tkc * TKW:(tkc + 1) * TKW]
            W("vector", self.p_ao)
            W("vector", self.p_res)
            for i2 in range(DCH):
                see("vector", acc_tok[i2])
            t_c1 = sig("vector", lambda v, accv=accv, attv=attv: v.tensor_add(accv, accv, attv))
            see("vector", t_c1)
            t_c2 = sig("vector", lambda v, accv=accv, resv=resv: v.tensor_add(accv, accv, resv))
            if L == 1:
                see("vector", t_c2)
                t_c2 = sig("vector", lambda v, accv=accv, xnv=xnv: v.tensor_add(accv, accv, xnv))
            tok_cmb_prev = t_c2
            see("sync", t_c2)
            if L == 0:
                dst = self.x1_scr[s, tkc]
                out_ap = SN.acc[:, :].rearrange("p (c t) -> p c t", c=DCH)[:, :, tkc * TKW:(tkc + 1) * TKW]
                tok_st = dma("sync", lambda sy, dst=dst, out_ap=out_ap: sy.dma_start(
                    out=dst.rearrange("p (c t) -> p c t", c=DCH), in_=out_ap), self.p_out)
            else:
                dst = self.out[s].rearrange("(c p) t -> p c t", p=128)[:, :, tkc * TKW:(tkc + 1) * TKW]
                out_ap = SN.acc[:, :].rearrange("p (c t) -> p c t", c=DCH)[:, :, tkc * TKW:(tkc + 1) * TKW]
                tok_st = dma("sync", lambda sy, dst=dst, out_ap=out_ap: sy.dma_start(
                    out=dst, in_=out_ap), self.p_out)
            tok_out_prev[par] = tok_st

    def finish(self):
        E, sig, see, W = self.E, self.sig, self.see, self.W
        self.quiesce()
        for nm in list(self.dbg):
            src = {"w0T": lambda: self.w0T[0], "w0T1": lambda: self.w0T[1],
                   "wmean": lambda: self.wmean, "stats": lambda: self.stats_gl,
                   "stats_loc": lambda: self.stats_loc}.get(nm)
            if src is None:
                continue
            t = src()
            self.dma("gpsimd", lambda g, t=t, nm=nm: g.dma_start(out=self.dbg[nm], in_=t[:, :]),
                     self.p_dbg)
        W("gpsimd", self.p_dbg)
        W("sync", self.p_out)
        W("sync", self.p_xn_st)

def build_program(debug_names=(), stage="prepass", serialize=False):
    nc = bass.Bass("TRN2", target_bir_lowering=False, detect_race_conditions=False)
    if os.environ.get("BIG_SBUF") == "1":
        nc.sbuf_top = 4_000_000
    st = ExitStack()
    with st:
        b = Build(nc, st, debug_names, serialize=serialize)
        b.declare_io()
        if "w0T" in b.debug_names:
            b.dout("w0T", [16, T]); b.dout("w0T1", [16, T])
        if "wmean" in b.debug_names:
            b.dout("wmean", [8, 1])
        if "stats" in b.debug_names:
            b.dout("stats", [8, 2])
        for nm, shape, dt in [("xb", [128, 4096], BF16), ("xq", [128, 4096], BF16),
                              ("mrow", [1, 512], F32), ("erow", [1, 512], F32),
                              ("vrow", [1, 512], F32), ("rrow", [1, 512], F32),
                              ("sct", [16, 512], F32), ("stats_loc", [128, 32], F32),
                              ("xnb", [128, 4096], BF16), ("xc0", [128, 4096], F32)]:
            if nm in b.debug_names:
                b.dout(nm, shape, dt)
        b.alloc()
        if os.environ.get("BIG_SBUF") == "1":
            probe = st.enter_context(nc.sbuf_tensor("addrprobe", [128, 32], mybir.dt.uint8))
            base = nc.lookup_mloc(probe).addr + 32
            st.enter_context(nc.sbuf_tensor("bigpad", [128, 229368 - base], mybir.dt.uint8))
        b.load_consts()
        nseq = int(os.environ.get("PP_SEQS", str(S)))
        b.open_scope(b.alloc_pre)
        for s in range(nseq):
            b.prepass_seq(0, s)
        if os.environ.get("SKIP_CC") != "1" and os.environ.get("SKIP_ROUTER") != "1" and nseq == S:
            b.collective_wmean()
        b.close_scope()
        nlayer = int(os.environ.get("STAGE_LAYERS", "2"))
        for L in range(nlayer):
            if L == 1:
                b.open_scope(b.alloc_pre)
                for s in range(S):
                    b.prepass_seq(1, s)
                b.close_scope()
            for s in range(S):
                b.open_scope(b.alloc_att)
                b.phase_att(L, s)
                b.close_scope()
                b.open_scope(b.alloc_ffn)
                b.phase_ffn(L, s)
                b.close_scope()
        b.finish()
        if os.environ.get("BIG_SBUF") == "1":
            while getattr(b, "_scopes", []):
                b._scopes.pop().close()
        with nc.Block() as block:
            @block.sync
            def _(sy):
                b.drain("sync", sy)
            @block.gpsimd
            def _(g):
                b.drain("gpsimd", g)
            @block.tensor
            def _(te):
                b.drain("tensor", te)
            @block.scalar
            def _(se):
                b.drain("scalar", se)
            @block.vector
            def _(ve):
                b.drain("vector", ve)
    return nc, b


def kernel(**inputs):
    g = host_layout(inputs)
    nc, b = build_program()
    in_maps = []
    for c in range(NCORES):
        m = dict(g)
        m["xT"] = g["xT"][2 * c:2 * c + 2]
        in_maps.append(m)
    res = run_bass_kernel_spmd(nc, in_maps, core_ids=list(range(NCORES)))
    outs = [np.asarray(r["out"], dtype=np.float32) for r in res.results]
    full = np.concatenate(outs, axis=0)
    return np.ascontiguousarray(full.transpose(0, 2, 1))


# version 10 (final)
# speedup vs baseline: 1.0944x; 1.0944x over previous
import os
import numpy as np
import ml_dtypes
from contextlib import ExitStack

import concourse.bass as bass
import concourse.mybir as mybir
from concourse.bass_utils import run_bass_kernel_spmd

F32 = mybir.dt.float32
BF16 = mybir.dt.bfloat16
AF = mybir.ActivationFunctionType
ALU = mybir.AluOpType
AX = mybir.AxisListType

NCORES = 8
S = 2
T = 2048
D = 1024
DCH = 8
H = 16
DH = 64
QCW = 512
NQC = T // QCW
NKT = T // 128
TKW = 256
NTK = T // TKW
I_EXP = 512
NEG_BIG = -1e30
ENGS = ("sync", "gpsimd", "tensor", "scalar", "vector")


def _bfa(x):
    return np.ascontiguousarray(x).astype(ml_dtypes.bfloat16)


def _f(x):
    return np.ascontiguousarray(x, dtype=np.float32)


def host_layout(inputs):
    g = {}
    x = _f(inputs["L0_attn_x"])
    g["xT"] = np.ascontiguousarray(x.transpose(0, 2, 1))

    def wT(w):
        return _f(np.ascontiguousarray(np.asarray(w).T).reshape(DCH, 128, -1))

    def bias_col(b):
        return _f(np.ascontiguousarray(np.asarray(b).reshape(DCH, 128).T))

    perm = np.concatenate([np.concatenate([h * 64 + np.arange(0, 64, 2),
                                           h * 64 + np.arange(1, 64, 2)])
                           for h in range(H)])

    g["l0_wq"] = wT(inputs["L0_attn_Wq"]); g["l0_bq"] = bias_col(inputs["L0_attn_bq"])
    g["l0_wk"] = wT(inputs["L0_attn_Wk"]); g["l0_bk"] = bias_col(inputs["L0_attn_bk"])
    g["l0_wv"] = wT(inputs["L0_attn_Wv"]); g["l0_bv"] = _f(inputs["L0_attn_bv"]).reshape(1, D)
    g["l0_wo"] = wT(inputs["L0_attn_Wo"]); g["l0_bo"] = bias_col(inputs["L0_attn_bo"])
    wq1 = np.asarray(_f(inputs["L1_attn_Wq"]))[perm]
    wk1 = np.asarray(_f(inputs["L1_attn_Wk"]))[perm]
    g["l1_wq"] = wT(wq1); g["l1_bq"] = bias_col(np.asarray(_f(inputs["L1_attn_bq"]))[perm])
    g["l1_wk"] = wT(wk1); g["l1_bk"] = bias_col(np.asarray(_f(inputs["L1_attn_bk"]))[perm])
    g["l1_wv"] = wT(inputs["L1_attn_Wv"]); g["l1_bv"] = _f(inputs["L1_attn_bv"]).reshape(1, D)
    g["l1_wp"] = wT(inputs["L1_attn_Wp"]); g["l1_bp"] = bias_col(inputs["L1_attn_bp"])

    g["l0_gate"] = _f(np.ascontiguousarray(np.asarray(_f(inputs["L0_ffn_gate_w"])).T).reshape(DCH, 128, 8))
    g["l0_gbias"] = _f(inputs["L0_ffn_lb_bias"]).reshape(8, 1)
    g["l1_gate"] = _f(np.ascontiguousarray(_f(inputs["L1_ffn_gate_w"])).reshape(DCH, 128, 16))
    g["l1_gbias"] = _f(inputs["L1_ffn_gate_b"]).reshape(16, 1)

    e0g = np.ascontiguousarray(_f(inputs["L0_ffn_expert_gate_w"]).transpose(0, 2, 1))
    e0u = np.ascontiguousarray(_f(inputs["L0_ffn_expert_up_w"]).transpose(0, 2, 1))
    g["e0_gu"] = _f(np.stack([e0g, e0u], axis=1).reshape(8, 2, DCH, 128, I_EXP))
    e0d = np.ascontiguousarray(_f(inputs["L0_ffn_expert_down_w"]).transpose(0, 2, 1))
    g["e0_d"] = _f(e0d.reshape(8, 4, 128, D))
    g["s0_g"] = wT(inputs["L0_ffn_shared_gate_w"])
    g["s0_u"] = wT(inputs["L0_ffn_shared_up_w"])
    g["s0_d"] = _f(np.ascontiguousarray(_f(inputs["L0_ffn_shared_down_w"]).T).reshape(DCH, 128, D))

    g["e1_gu"] = _f(np.stack([_f(inputs["L1_ffn_experts_wg"]),
                              _f(inputs["L1_ffn_experts_wu"])], axis=1).reshape(16, 2, DCH, 128, I_EXP))
    g["e1_d"] = _f(_f(inputs["L1_ffn_experts_wd"]).reshape(16, 4, 128, D))
    g["s1_gu"] = _f(np.stack([_f(inputs["L1_ffn_shared_wg"]),
                              _f(inputs["L1_ffn_shared_wu"])], axis=0).reshape(2, DCH, 128, I_EXP))
    g["s1_d"] = _f(_f(inputs["L1_ffn_shared_wd"]).reshape(4, 128, D))

    j = np.arange(32, dtype=np.float64)
    invf = 1.0 / (10000.0 ** (2.0 * j / 64.0))
    t = np.arange(T, dtype=np.float64)
    ang = t[None, :] * invf[:, None]
    cos = np.cos(ang); sin = np.sin(ang)
    g["cosin"] = _f(np.stack([np.concatenate([cos, cos], axis=0),
                              np.concatenate([-sin, sin], axis=0),
                              np.concatenate([sin, -sin], axis=0)], axis=0))
    g["eye"] = _f(np.eye(128))
    sel = np.zeros((16, 16, 128), np.float32)
    for e in range(16):
        sel[e, e, :] = 1.0
    g["selbank"] = _f(sel.transpose(1, 0, 2).reshape(16, 16 * 128))
    i_ = np.arange(128)[:, None]; j_ = np.arange(512)[None, :]
    mk = np.stack([(j_ >= i_ + m * 128) for m in range(4)], axis=0).astype(np.float32)
    g["masks"] = _f(np.concatenate([mk, mk], axis=2))
    return g


class Pipe:
    def __init__(self, st, nc, name):
        self.sem = st.enter_context(nc.semaphore(name))
        self.count = 0
        self.name = name


class Bank:
    def __init__(self, st, nc, idx, cols=512):
        self.idx = idx
        self.t = st.enter_context(nc.psum_tensor(
            f"pbank{idx}_{nc.next_id()}", [128, cols], F32, unsafe_allow_realloc=True))
        self.mm_tok = None
        self.free_tok = None


class Build:

    def __init__(self, nc, st, debug_names=(), serialize=False):
        self.nc = nc
        self.st = st
        self.debug_names = set(debug_names)
        self.dbg = {}
        self.q = {e: [] for e in ENGS}
        self.tail_signed = {e: True for e in ENGS}
        self.serialize = serialize

    def snap(self, names):
        from types import SimpleNamespace
        return SimpleNamespace(**{n: getattr(self, n) for n in names.split()})

    def E(self, engname, fn):
        self.q[engname].append(("op", fn, None, 0))
        self.tail_signed[engname] = False

    def sig(self, engname, fn):
        pipe = self.pp[engname]
        pipe.count += 1
        self.q[engname].append(("op", fn, pipe, 1))
        self.tail_signed[engname] = True
        return (pipe, pipe.count)

    def dma(self, engname, fn, pipe):
        pipe.count += 16
        self.q[engname].append(("op", fn, pipe, 16))
        self.tail_signed[engname] = True
        return (pipe, pipe.count)

    def see(self, engname, tok):
        if tok is not None:
            self.q[engname].append(("wait", tok[0], tok[1]))

    def W(self, engname, pipe, count=None):
        c = pipe.count if count is None else count
        if c > 0:
            self.q[engname].append(("wait", pipe, c))

    def drain(self, engname, eng):
        for item in self.q[engname]:
            if item[0] == "wait":
                _, pipe, c = item
                eng.wait_ge(pipe.sem, c)
            else:
                _, fn, produce, n = item
                instr = fn(eng)
                if produce is not None:
                    instr.then_inc(produce.sem, n)

    def declare_io(self):
        nc = self.nc
        def din(name, shape):
            return nc.dram_tensor(name, list(shape), F32, kind="ExternalInput").ap()
        self.i = {}
        for nm, shape in [
            ("xT", [S, D, T]),
            ("l0_wq", [DCH, 128, D]), ("l0_wk", [DCH, 128, D]), ("l0_wv", [DCH, 128, D]),
            ("l0_wo", [DCH, 128, D]), ("l1_wq", [DCH, 128, D]), ("l1_wk", [DCH, 128, D]),
            ("l1_wv", [DCH, 128, D]), ("l1_wp", [DCH, 128, D]),
            ("l0_bq", [128, 8]), ("l0_bk", [128, 8]), ("l0_bo", [128, 8]),
            ("l1_bq", [128, 8]), ("l1_bk", [128, 8]), ("l1_bp", [128, 8]),
            ("l0_bv", [1, D]), ("l1_bv", [1, D]),
            ("l0_gate", [DCH, 128, 8]), ("l0_gbias", [8, 1]),
            ("l1_gate", [DCH, 128, 16]), ("l1_gbias", [16, 1]),
            ("e0_gu", [8, 2, DCH, 128, I_EXP]), ("e0_d", [8, 4, 128, D]),
            ("s0_g", [DCH, 128, D]), ("s0_u", [DCH, 128, D]), ("s0_d", [DCH, 128, D]),
            ("e1_gu", [16, 2, DCH, 128, I_EXP]), ("e1_d", [16, 4, 128, D]),
            ("s1_gu", [2, DCH, 128, I_EXP]), ("s1_d", [4, 128, D]),
            ("cosin", [3, 64, T]), ("eye", [128, 128]), ("selbank", [16, 16 * 128]),
            ("masks", [4, 128, 1024]),
        ]:
            self.i[nm] = din(nm, shape)
        self.out = nc.dram_tensor("out", [S, D, T], F32, kind="ExternalOutput").ap()
        k = lambda nm: "ExternalOutput" if nm in self.debug_names else "Internal"
        self.xn_scr = nc.dram_tensor("xn_scr", [S, NQC, 128, DCH * QCW], BF16, kind=k("xn_scr")).ap()
        self.x1_scr = nc.dram_tensor("x1_scr", [S, NTK, 128, DCH * TKW], F32, kind=k("x1_scr")).ap()
        self.ao_scr = nc.dram_tensor("ao_scr", [NQC, 128, DCH * QCW], BF16, kind=k("ao_scr")).ap()
        self.cc_in = nc.dram_tensor("cc_in", [128, 32], F32)
        self.cc_out = nc.dram_tensor("cc_out", [128, 32], F32)

    def dout(self, name, shape, dt=F32):
        if name in self.debug_names:
            self.dbg[name] = self.nc.dram_tensor("dbg_" + name, list(shape), dt,
                                                 kind="ExternalOutput").ap()
            return self.dbg[name]
        return None

    def alloc(self):
        nc, st = self.nc, self.st
        sb = lambda name, shape, dt: st.enter_context(nc.sbuf_tensor(name, shape, dt))
        self.cos = sb("cos", [128, T], BF16)
        self.ssin = sb("ssin", [128, T], BF16)
        self.ssin2 = sb("ssin2", [128, T], BF16)
        self.eye_f = sb("eye_f", [128, 128], F32)
        self.eye_b = sb("eye_b", [128, 128], BF16)
        self.nlb0 = sb("nlb0", [8, 1], F32)
        self.nlb1 = sb("nlb1", [16, 1], F32)
        self.bias_s = {}
        for nm in ["l0_bq", "l0_bk", "l0_bo", "l1_bq", "l1_bk", "l1_bp"]:
            self.bias_s[nm] = sb(nm + "_s", [128, 8], F32)
        self.ones_b = sb("ones_b", [128, 1], BF16)
        self.ones_f = sb("ones_f", [128, 1], F32)
        self.ones_r = sb("ones_r", [1, 128], F32)
        self.ones_rb = sb("ones_rb", [1, 128], BF16)
        self.ones_p64 = sb("ones_p64", [65, 64], BF16)
        self.eps_ln = sb("eps_ln", [1, 1], F32)
        self.w0T = [sb(f"w0T{s}", [16, T], BF16) for s in range(S)]
        self.w1T = [sb(f"w1T{s}", [16, T], BF16) for s in range(S)]
        self.msel_st = [sb(f"msel{s}", [128, 16 * 8], F32) for s in range(S)]
        self.stats_loc = sb("stats_loc", [128, 32], F32)
        self.stats_gl = sb("stats_gl", [8, 2], F32)
        self.wmean = sb("wmean", [8, 1], F32)
        self.wm_c = sb("wm_c", [8, 1], F32)
        self.wm_rc = sb("wm_rc", [8, 1], F32)
        self.wslot = [sb(f"wslot{k}", [128, 8192], BF16) for k in range(3)]
        self.qmark = sb("qmark", [1, 2], F32)
        self.wslot_free = [None, None, None]
        self.wslot_tok = [None, None, None]
        P = lambda name: Pipe(st, nc, name)
        self.p_const = P("p_const")
        self.p_w = P("p_w")
        self.p_xin = P("p_xin")
        self.p_xn_st = P("p_xn_st")
        self.p_xn_ld = P("p_xn_ld")
        self.p_res = P("p_res")
        self.p_out = P("p_out")
        self.p_ao = P("p_ao")
        self.p_cc = P("p_cc")
        self.p_ccc = P("p_ccc")
        self.p_dbg = P("p_dbg")
        self.pp = {e: P("pp_" + e) for e in ENGS}
        self.tok = {}

    def quiesce(self):
        if not self.tail_signed["vector"]:
            self.sig("vector", lambda v: v.memset(self.qmark[0:1, 0:1], 0.0))
        if not self.tail_signed["scalar"]:
            self.sig("scalar", lambda se: se.activation(self.qmark[0:1, 1:2],
                                                        self.qmark[0:1, 0:1], AF.Copy))
        assert self.tail_signed["tensor"], "tensor queue tail must be signalled"
        assert self.tail_signed["gpsimd"], "gpsimd queue tail must be signalled"
        pipes = [self.p_const, self.p_w, self.p_xin, self.p_xn_st, self.p_xn_ld,
                 self.p_res, self.p_out, self.p_ao, self.p_cc, self.p_ccc,
                 self.p_dbg] + [self.pp[e] for e in ENGS]
        for e in ENGS:
            for p in pipes:
                self.W(e, p)

    def open_scope(self, alloc_fn):
        st2 = ExitStack()
        self._scopes = getattr(self, "_scopes", [])
        self._scopes.append(st2)
        alloc_fn(st2)

    def close_scope(self):
        self.quiesce()
        if os.environ.get("BIG_SBUF") == "1":
            pass
        else:
            self._scopes.pop().close()

    def alloc_pre(self, st):
        nc = self.nc
        sb = lambda name, shape, dt: st.enter_context(nc.sbuf_tensor(
            name + str(nc.next_id()), shape, dt, unsafe_allow_realloc=True))
        self.banks = [Bank(st, nc, i) for i in range(8)]
        g0f = self.g0f = sb("g0f", [128, DCH * 8], F32)
        g1f = self.g1f = sb("g1f", [128, DCH * 16], F32)
        self.dma("gpsimd", lambda g, g0f=g0f: g.dma_start(
            out=g0f[:, :].rearrange("p (c e) -> p c e", c=DCH),
            in_=self.i["l0_gate"].transpose([1, 0, 2])), self.p_const)
        self.dma("gpsimd", lambda g, g1f=g1f: g.dma_start(
            out=g1f[:, :].rearrange("p (c e) -> p c e", c=DCH),
            in_=self.i["l1_gate"].transpose([1, 0, 2])), self.p_const)
        for e in ENGS:
            self.W(e, self.p_const)
        self.xc = [sb("xc", [128, DCH * QCW], F32) for p in range(2)]
        self.xb = sb("xb", [128, DCH * QCW], BF16)
        self.xq = sb("xq", [128, DCH * QCW], BF16)
        self.xnb = sb("xnb", [128, DCH * QCW], BF16)
        self.xnf = sb("xnf", [128, DCH * QCW], F32)
        self.mrow = sb("mrow", [1, QCW], F32)
        self.erow = sb("erow", [1, QCW], F32)
        self.vrow = sb("vrow", [1, QCW], F32)
        self.rrow = sb("rrow", [1, QCW], F32)
        self.tmp512 = sb("tmp512", [128, QCW], F32)
        self.tmp512b = sb("tmp512b", [128, QCW], F32)
        self.sct = sb("sct", [16, QCW], F32)
        self.sct2 = sb("sct2", [16, QCW], F32)
        self.sctok4 = sb("sctok4", [128, 4 * 16], F32)
        self.t2 = {}
        for nm in ["m1", "m2", "s2", "w1", "w2", "wtok"]:
            self.t2[nm] = sb("t2" + nm, [128, 4 * 16], F32)
        for nm in ["max1", "max2", "den", "rden", "a1", "a2"]:
            self.t2[nm] = sb("t2" + nm, [128, 4], F32)

    def alloc_att(self, st):
        nc = self.nc
        sb = lambda name, shape, dt: st.enter_context(nc.sbuf_tensor(
            name + str(nc.next_id()), shape, dt, unsafe_allow_realloc=True))
        self.abank = [Bank(st, nc, 100 + i) for i in range(4)]
        self.psc2 = [Bank(st, nc, 200 + p, cols=1024) for p in range(2)]
        maskb = self.maskb = sb("maskb", [128, 4 * 1024], BF16)
        self.dma("gpsimd", lambda g, maskb=maskb: g.dma_start(
            out=maskb[:, :].rearrange("p (m w) -> p m w", m=4),
            in_=self.i["masks"].transpose([1, 0, 2])), self.p_const)
        for e in ("vector", "scalar", "tensor"):
            self.W(e, self.p_const)
        self.KT = sb("KT", [128, DCH * T], BF16)
        self.Vp = sb("Vp", [128, NKT * (H * 65)], BF16)
        self.QTc = sb("QTc", [128, DCH * QCW], BF16)
        self.och = sb("och", [128, DCH * QCW], BF16)
        self.xn_ch = [sb("xn_ch", [128, DCH * QCW], BF16) for p in range(2)]
        self.bvrow = sb("bvrow", [1, D], BF16)
        self.bvb = sb("bvb", [128, D], BF16)
        self.k_bf = sb("k_bf", [128, QCW], BF16)
        self.q_bf = sb("q_bf", [128, QCW], BF16)
        self.rc1 = sb("rc1", [128, QCW], BF16)
        self.rc2 = sb("rc2", [128, QCW], BF16)
        self.exp2 = [sb("exp2", [128, 2 * QCW], BF16) for p in range(2)]
        self.oraw = sb("oraw", [65, 4 * QCW], BF16)
        self.lrow = [sb("lrow", [1, QCW], F32) for _ in range(2)]
        self.lrinv = [sb("lrinv", [1, QCW], BF16) for _ in range(2)]
        self.linv_b = sb("linv_b", [64, 2 * QCW], BF16)

    def alloc_ffn(self, st):
        nc = self.nc
        sb = lambda name, shape, dt: st.enter_context(nc.sbuf_tensor(
            name + str(nc.next_id()), shape, dt, unsafe_allow_realloc=True))
        self.fbank = [Bank(st, nc, 300 + i) for i in range(8)]
        selb = self.selb = sb("selb", [16, 16 * 128], BF16)
        self.dma("gpsimd", lambda g, selb=selb: g.dma_start(out=selb[:, :], in_=self.i["selbank"]),
                 self.p_const)
        for e in ("vector", "scalar", "tensor"):
            self.W(e, self.p_const)
        self.xn_s = sb("xn_s", [128, DCH * T], BF16)
        self.acc = sb("acc", [128, DCH * T], F32)
        self.att_tk = [sb("att_tk", [128, DCH * TKW], BF16)]
        self.xres_tk = [sb("xres_tk", [128, DCH * TKW], F32)]
        self.m_buf = [sb("m_buf", [128, 8 * TKW], BF16) for p in range(2)]
        self.wbc_sb = sb("wbc_sb", [128, 2 * 2 * TKW], BF16)
        self.sg_sb = sb("sg_sb", [128, 2 * TKW], BF16)
        self.su_sb = sb("su_sb", [128, 2 * TKW], BF16)

    def load_consts(self):
        i = self.i
        skip = os.environ.get("SKIP_CONSTS") == "1"
        def d(out, in_):
            if skip:
                return
            self.dma("gpsimd", lambda g, out=out, in_=in_: g.dma_start(out=out, in_=in_),
                     self.p_const)
        d(self.cos[0:64, :], i["cosin"][0])
        d(self.cos[64:128, :], i["cosin"][0])
        d(self.ssin[0:64, :], i["cosin"][1])
        d(self.ssin[64:128, :], i["cosin"][1])
        d(self.ssin2[0:64, :], i["cosin"][2])
        d(self.ssin2[64:128, :], i["cosin"][2])
        d(self.eye_f[:, :], i["eye"])
        d(self.eye_b[:, :], i["eye"])
        d(self.nlb0[:, :], i["l0_gbias"])
        d(self.nlb1[:, :], i["l1_gbias"])
        for nm in self.bias_s:
            d(self.bias_s[nm][:, :], i[nm])
        for tile, val in [(self.ones_b, 1.0), (self.ones_f, 1.0), (self.ones_r, 1.0),
                          (self.ones_rb, 1.0), (self.ones_p64, 1.0),
                          (self.eps_ln, 1e-5), (self.stats_loc, 0.0),
                          (self.w0T[0], 0.0), (self.w0T[1], 0.0), (self.w1T[0], 0.0), (self.w1T[1], 0.0)]:
            self.E("vector", lambda ve, t=tile, v=val: ve.memset(t[:, :], v))
        self.W("vector", self.p_const)
        self.E("vector", lambda v: v.tensor_scalar_mul(self.nlb0[:, :], self.nlb0[:, :], -1.0))
        tok = self.sig("vector",
                       lambda v: v.tensor_scalar_mul(self.nlb1[:, :], self.nlb1[:, :], -1.0))
        for e in ENGS:
            self.W(e, self.p_const)
            if e != "vector":
                self.see(e, tok)

    def prepass_seq(self, layer, s):
        SN = self.snap("xc xb xq xnb xnf mrow erow vrow rrow tmp512 tmp512b sct sct2 sctok4 t2")
        """LN stats -> xn (fp32 + bf16, stored to HBM) -> fp32 router sigmoid ->
        top-2. layer 0: accumulate global (S, C) stats + keep msel; layer 1:
        build w1T. Same-engine dependent DVE/ACT pairs are guarded with tokens."""
        E, sig, see, W = self.E, self.sig, self.see, self.W
        Elay = 8 if layer == 0 else 16
        nlb = self.nlb0 if layer == 0 else self.nlb1
        gatef = self.g0f if layer == 0 else self.g1f
        bS, bE, bM, bR, bB0, bB1, bT, bW_ = self.banks
        t2 = SN.t2
        tk = self.tok

        for tc in range(NQC):
            par = tc % 2
            xcp = SN.xc[par]
            see("sync", tk.get(("xc_free", par)))
            if layer == 0:
                src = self.i["xT"][s].rearrange("(c p) t -> p c t", p=128)[:, :, tc * QCW:(tc + 1) * QCW]
                self.dma("sync", lambda sy, src=src, xcp=xcp: sy.dma_start(
                    out=xcp[:, :].rearrange("p (c t) -> p c t", c=DCH), in_=src), self.p_xin)
            else:
                for h2 in range(2):
                    src = self.x1_scr[s, tc * 2 + h2].rearrange("p (c u) -> p c u", c=DCH)
                    dst = xcp[:, :].rearrange("p (c t) -> p c t", c=DCH)[:, :, h2 * TKW:(h2 + 1) * TKW]
                    self.dma("sync", lambda sy, src=src, dst=dst: sy.dma_start(out=dst, in_=src),
                             self.p_xin)
            W("scalar", self.p_xin)
            see("scalar", tk.get("xb_free"))
            tok_cb = sig("scalar", lambda se, xcp=xcp: se.activation(SN.xb[:, :], xcp[:, :], AF.Copy))
            tok_sc = sig("scalar", lambda se, xcp=xcp: se.activation(SN.xq[:, :], xcp[:, :], AF.Square))
            see("tensor", tok_cb)
            see("tensor", bS.free_tok)
            for d in range(DCH):
                fs = (d == 0); ls = (d == DCH - 1)
                mS = lambda te, d=d, fs=fs, ls=ls: te.matmul(
                    bS.t[0:1, 0:QCW], self.ones_b[:, 0:1],
                    SN.xb[:, d * QCW:(d + 1) * QCW], start=fs, stop=ls)
                if ls:
                    tok_mmS = sig("tensor", mS)
                else:
                    E("tensor", mS)
            see("tensor", tok_sc)
            see("tensor", bE.free_tok)
            for d in range(DCH):
                fs = (d == 0); ls = (d == DCH - 1)
                mE = lambda te, d=d, fs=fs, ls=ls: te.matmul(
                    bE.t[0:1, 0:QCW], self.ones_b[:, 0:1],
                    SN.xq[:, d * QCW:(d + 1) * QCW], start=fs, stop=ls)
                if ls:
                    tok_mm = sig("tensor", mE)
                else:
                    E("tensor", mE)
            bS.mm_tok = tok_mmS
            bE.mm_tok = tok_mm
            tk["xb_free"] = tok_mmS
            see("vector", tok_mmS)
            see("vector", tok_mm)
            see("vector", tk.get("rows_free"))
            tok_a = sig("vector", lambda v: v.tensor_scalar_mul(SN.mrow[:, :], bS.t[0:1, 0:QCW], 1.0 / D))
            tok_b = sig("vector", lambda v: v.tensor_scalar_mul(SN.erow[:, :], bE.t[0:1, 0:QCW], 1.0 / D))
            see("vector", tok_a)
            tok_c = sig("vector", lambda v: v.tensor_mul(SN.vrow[:, :], SN.mrow[:, :], SN.mrow[:, :]))
            see("vector", tok_c)
            tok_rows = sig("vector", lambda v: v.tensor_sub(SN.vrow[:, :], SN.erow[:, :], SN.vrow[:, :]))
            bS.free_tok = bE.free_tok = tok_rows
            see("scalar", tok_rows)
            tok_ln = sig("scalar", lambda se: se.activation(SN.rrow[:, :], SN.vrow[:, :], AF.Ln,
                                                            bias=self.eps_ln[0:1, 0:1]))
            see("scalar", tok_ln)
            tok_rstd = sig("scalar", lambda se: se.activation(SN.rrow[:, :], SN.rrow[:, :],
                                                              AF.Exp, scale=-0.5))
            see("tensor", tok_rstd)
            see("tensor", bB0.free_tok)
            see("tensor", bB1.free_tok)
            bB0.mm_tok = sig("tensor", lambda te: te.matmul(
                bB0.t[:, 0:QCW], self.ones_r[0:1, :], SN.mrow[0:1, :], start=True, stop=True))
            bB1.mm_tok = sig("tensor", lambda te: te.matmul(
                bB1.t[:, 0:QCW], self.ones_r[0:1, :], SN.rrow[0:1, :], start=True, stop=True))
            tk["rows_free"] = bB1.mm_tok
            see("vector", bB1.mm_tok)
            see("vector", tk.get("xnf_free"))
            tmps = [SN.tmp512, SN.tmp512b]
            sub_toks = [None, None]
            mul_toks = [None, None]
            for d in range(DCH):
                tb = tmps[d % 2]
                see("vector", mul_toks[d % 2])
                sub_toks[d % 2] = sig("vector", lambda v, d=d, xcp=xcp, tb=tb: v.tensor_sub(
                    tb[:, :], xcp[:, d * QCW:(d + 1) * QCW], bB0.t[:, 0:QCW]))
                see("vector", sub_toks[d % 2])
                mul_toks[d % 2] = sig("vector", lambda v, d=d, tb=tb: v.tensor_mul(
                    SN.xnf[:, d * QCW:(d + 1) * QCW], tb[:, :], bB1.t[:, 0:QCW]))
            tok_xnf = mul_toks[(DCH - 1) % 2]
            bB0.free_tok = bB1.free_tok = tok_xnf
            tk[("xc_free", par)] = tok_xnf
            see("scalar", tok_xnf)
            see("scalar", tk.get("xnb_free"))
            W("scalar", self.p_xn_st)
            tok_xnb = sig("scalar", lambda se: se.activation(SN.xnb[:, :], SN.xnf[:, :], AF.Copy))
            see("sync", tok_xnb)
            self.dma("sync", lambda sy, s=s, tc=tc: sy.dma_start(
                out=self.xn_scr[s, tc], in_=SN.xnb[:, :]), self.p_xn_st)
            if os.environ.get("SKIP_ROUTER") == "1":
                tk["xnb_free"] = None
                tk["xnf_free"] = None
                continue
            see("tensor", tok_xnf)
            see("tensor", bM.free_tok)
            for d in range(DCH):
                fn = lambda te, d=d: te.matmul(
                    bM.t[0:Elay, 0:QCW], gatef[:, d * Elay:(d + 1) * Elay],
                    SN.xnf[:, d * QCW:(d + 1) * QCW], start=(d == 0), stop=(d == DCH - 1))
                if d == DCH - 1:
                    bM.mm_tok = sig("tensor", fn)
                else:
                    E("tensor", fn)
            tk["xnf_free"] = bM.mm_tok
            tk["xnb_free"] = tok_xnb
            see("scalar", bM.mm_tok)
            see("scalar", tk.get("sct_free"))
            tok_sg = sig("scalar", lambda se: se.activation(
                SN.sct[0:Elay, :], bM.t[0:Elay, 0:QCW], AF.Exp,
                bias=nlb[0:Elay, 0:1], scale=-1.0))
            bM.free_tok = tok_sg
            see("vector", tok_sg)
            see("vector", tk.get("sct2_free"))
            tok_s1 = sig("vector", lambda v: v.tensor_scalar_add(SN.sct2[0:Elay, :],
                                                                 SN.sct[0:Elay, :], 1.0))
            see("vector", tok_s1)
            tok_sct = sig("vector", lambda v: v.reciprocal(SN.sct2[0:Elay, :], SN.sct2[0:Elay, :]))
            tk["sct_free"] = tok_s1
            see("tensor", tok_sct)
            see("tensor", bR.free_tok)
            for w in range(4):
                fn = lambda te, w=w: te.transpose(bR.t[:, w * 16:w * 16 + Elay],
                                                  SN.sct2[0:Elay, w * 128:(w + 1) * 128],
                                                  self.eye_f[0:Elay, 0:Elay])
                if w == 3:
                    bR.mm_tok = sig("tensor", fn)
                else:
                    E("tensor", fn)
            tk["sct2_free"] = bR.mm_tok
            see("vector", bR.mm_tok)
            see("vector", tk.get("t2_free"))
            def step(name, fn_w):
                tok = None
                for w in range(4):
                    tok = sig("vector", lambda v, w=w: fn_w(v, w))
                return tok
            co = lambda w: slice(w * 16, w * 16 + Elay)
            c1 = lambda w: slice(w, w + 1)
            tok_cp = step("cp", lambda v, w: v.tensor_copy(SN.sctok4[:, co(w)], bR.t[:, w * 16:w * 16 + Elay]))
            see("vector", tok_cp)
            tok_m1 = step("max1", lambda v, w: v.reduce_max(t2["max1"][:, c1(w)], SN.sctok4[:, co(w)], AX.X))
            see("vector", tok_m1)
            tok_ms1 = step("m1", lambda v, w: v.tensor_single_scalar(
                t2["m1"][:, co(w)], SN.sctok4[:, co(w)], t2["max1"][:, c1(w)], ALU.is_ge))
            see("vector", tok_ms1)
            tok_s2 = step("s2", lambda v, w: v.scalar_tensor_tensor(
                t2["s2"][:, co(w)], t2["m1"][:, co(w)], NEG_BIG, SN.sctok4[:, co(w)],
                ALU.mult, ALU.add))
            see("vector", tok_s2)
            tok_m2 = step("max2", lambda v, w: v.reduce_max(t2["max2"][:, c1(w)], t2["s2"][:, co(w)], AX.X))
            see("vector", tok_m2)
            tok_ms2 = step("m2", lambda v, w: v.tensor_single_scalar(
                t2["m2"][:, co(w)], t2["s2"][:, co(w)], t2["max2"][:, c1(w)], ALU.is_ge))
            see("vector", tok_ms2)
            tok_dn = step("den", lambda v, w: v.tensor_add(t2["den"][:, c1(w)], t2["max1"][:, c1(w)],
                                                           t2["max2"][:, c1(w)]))
            if layer == 0:
                see("vector", tok_dn)
                tok_dn = step("dne", lambda v, w: v.tensor_scalar_add(t2["den"][:, c1(w)],
                                                                     t2["den"][:, c1(w)], 1e-8))
            see("vector", tok_dn)
            tok_rd = step("rden", lambda v, w: v.reciprocal(t2["rden"][:, c1(w)], t2["den"][:, c1(w)]))
            see("vector", tok_rd)
            tok_a1 = step("a1", lambda v, w: v.tensor_mul(t2["a1"][:, c1(w)], t2["max1"][:, c1(w)],
                                                          t2["rden"][:, c1(w)]))
            see("vector", tok_a1)
            tok_a2 = step("a2", lambda v, w: v.tensor_mul(t2["a2"][:, c1(w)], t2["max2"][:, c1(w)],
                                                          t2["rden"][:, c1(w)]))
            see("vector", tok_a2)
            tok_w1 = step("w1", lambda v, w: v.tensor_scalar_mul(t2["w1"][:, co(w)], t2["m1"][:, co(w)],
                                                                 t2["a1"][:, c1(w)]))
            see("vector", tok_w1)
            tok_w2 = step("w2", lambda v, w: v.tensor_scalar_mul(t2["w2"][:, co(w)], t2["m2"][:, co(w)],
                                                                 t2["a2"][:, c1(w)]))
            see("vector", tok_w2)
            tok_wt = step("wtok", lambda v, w: v.tensor_add(t2["wtok"][:, co(w)], t2["w1"][:, co(w)],
                                                            t2["w2"][:, co(w)]))
            bR.free_tok = tok_cp
            if layer == 0:
                see("vector", tok_ms2)
                tok_msel = None
                for w in range(4):
                    tile = tc * 4 + w
                    tok_msel = sig("vector", lambda v, s=s, tile=tile, w=w: v.tensor_add(
                        self.msel_st[s][:, tile * 8:tile * 8 + 8],
                        t2["m1"][:, w * 16:w * 16 + 8], t2["m2"][:, w * 16:w * 16 + 8]))
                see("tensor", tok_wt)
                see("tensor", tok_msel)
                tok_st = None
                for w in range(4):
                    tile = tc * 4 + w
                    first = (s == 0 and tile == 0)
                    last = (s == S - 1 and tile == 15)
                    E("tensor", lambda te, w=w, first=first: te.matmul(
                        bT.t[0:8, 0:1], t2["wtok"][:, w * 16:w * 16 + 8], self.ones_f[:, 0:1],
                        start=first, stop=False))
                    tok_st = sig("tensor", lambda te, s=s, tile=tile, last=last: te.matmul(
                        bT.t[0:8, 1:2], self.msel_st[s][:, tile * 8:tile * 8 + 8],
                        self.ones_f[:, 0:1], start=False, stop=last))
                    if last:
                        bT.mm_tok = tok_st
                tk["t2_free"] = tok_st
            else:
                see("tensor", tok_wt)
                tok_tr = None
                for w in range(4):
                    tile = tc * 4 + w
                    see("tensor", bW_.free_tok)
                    bW_.mm_tok = sig("tensor", lambda te, w=w: te.transpose(
                        bW_.t[0:16, 0:128], t2["wtok"][:, w * 16:w * 16 + 16], self.eye_f[:, :]))
                    tok_tr = bW_.mm_tok
                    see("vector", bW_.mm_tok)
                    bW_.free_tok = sig("vector", lambda v, tile=tile: v.tensor_copy(
                        self.w1T[s][0:16, tile * 128:(tile + 1) * 128], bW_.t[0:16, 0:128]))
                tk["t2_free"] = tok_tr

    def collective_wmean(self):
        E, sig, see, W = self.E, self.sig, self.see, self.W
        bT = self.banks[6]
        see("vector", bT.mm_tok)
        tok_ev = sig("vector", lambda v: v.tensor_copy(self.stats_loc[0:8, 0:2], bT.t[0:8, 0:2]))
        bT.free_tok = tok_ev
        see("gpsimd", tok_ev)
        self.dma("gpsimd", lambda g: g.dma_start(out=self.cc_in.ap(), in_=self.stats_loc[:, :]),
                 self.p_cc)
        W("gpsimd", self.p_cc)
        self.p_ccc.count += 1
        self.q["gpsimd"].append(("op", lambda g: g.collective_compute(
            "AllReduce", ALU.add, replica_groups=[list(range(NCORES))],
            ins=[self.cc_in.ap().opt()], outs=[self.cc_out.ap().opt()]), self.p_ccc, 1))
        W("gpsimd", self.p_ccc)
        self.dma("gpsimd", lambda g: g.dma_start(out=self.stats_gl[:, :],
                                                 in_=self.cc_out.ap()[0:8, 0:2]), self.p_cc)
        W("vector", self.p_cc)
        tok_1 = sig("vector", lambda v: v.tensor_scalar_max(self.wm_c[:, :], self.stats_gl[0:8, 1:2], 1.0))
        see("vector", tok_1)
        tok_2 = sig("vector", lambda v: v.reciprocal(self.wm_rc[:, :], self.wm_c[:, :]))
        see("vector", tok_2)
        tok_wm = sig("vector", lambda v: v.tensor_mul(self.wmean[:, :], self.stats_gl[0:8, 0:1],
                                                      self.wm_rc[:, :]))
        bW_ = self.banks[7]
        see("tensor", tok_wm)
        for s in range(S):
            for tile in range(16):
                see("tensor", bW_.free_tok)
                bW_.mm_tok = sig("tensor", lambda te, s=s, tile=tile: te.transpose(
                    bW_.t[0:8, 0:128], self.msel_st[s][:, tile * 8:tile * 8 + 8],
                    self.eye_f[:, :]))
                see("vector", bW_.mm_tok)
                see("vector", tok_wm)
                bW_.free_tok = sig("vector", lambda v, s=s, tile=tile: v.tensor_scalar_mul(
                    self.w0T[s][0:8, tile * 128:(tile + 1) * 128], bW_.t[0:8, 0:128],
                    self.wmean[0:8, 0:1]))

    def load_wslot(self, slot, dram_ap, cols_layout):
        see, dma = self.see, self.dma
        see("gpsimd", self.wslot_free[slot])
        if len(cols_layout) == 2:
            c, n = cols_layout
            out = self.wslot[slot][:, 0:c * n].rearrange("p (c n) -> p c n", c=c)
            in_ = dram_ap.transpose([1, 0, 2])
        else:
            g, c, n = cols_layout
            out = self.wslot[slot][:, 0:g * c * n].rearrange("p (g c n) -> p g c n", g=g, c=c)
            in_ = dram_ap.transpose([2, 0, 1, 3])
        tok = dma("gpsimd", lambda gp, out=out, in_=in_: gp.dma_start(out=out, in_=in_), self.p_w)
        self.wslot_tok[slot] = tok
        return tok

    def load_whalf(self, slot, half, dram_ap, tok_free):
        see, dma = self.see, self.dma
        see("gpsimd", tok_free)
        out = self.wslot[slot][:, half * 4096:(half + 1) * 4096].rearrange(
            "p (c n) -> p c n", c=4)
        in_ = dram_ap.transpose([1, 0, 2])
        return dma("gpsimd", lambda gp, out=out, in_=in_: gp.dma_start(out=out, in_=in_), self.p_w)

    def rope(self, src_bf, dst_ap, win, tok_src, tok_dst_free):
        SN = self.snap("rc1 rc2")
        """dst = src*cos + swap32(src)*ssin over a 512-token window.
        Returns token of the final add (frees src and rc scratch)."""
        sig, see = self.sig, self.see
        see("vector", tok_src)
        see("vector", self.tok.get("rope_fr"))
        see("vector", tok_dst_free)
        t1 = sig("vector", lambda v: v.tensor_mul(SN.rc1[:, :], src_bf[:, :],
                                                  self.cos[:, win]))
        toks2 = []
        for r0, r1 in ((0, 32), (32, 0), (64, 96), (96, 64)):
            toks2.append(sig("vector", lambda v, r0=r0, r1=r1: v.tensor_mul(
                SN.rc2[r0:r0 + 32, :], src_bf[r1:r1 + 32, :], self.ssin2[r1:r1 + 32, win])))
        see("vector", t1)
        see("vector", toks2[-1])
        t3 = sig("vector", lambda v, dst_ap=dst_ap: v.tensor_add(dst_ap, SN.rc1[:, :],
                                                                 SN.rc2[:, :]))
        self.tok["rope_fr"] = t3
        return t3

    def phase_att(self, L, s):
        SN = self.snap("KT Vp QTc och xn_ch k_bf q_bf exp2 oraw lrow lrinv linv_b bvrow bvb maskb")
        E_, sig, see, W, dma = self.E, self.sig, self.see, self.W, self.dma
        i = self.i
        pre = "l0_" if L == 0 else "l1_"
        wq_nm, wk_nm, wv_nm = pre + "wq", pre + "wk", pre + "wv"
        wo_nm = "l0_wo" if L == 0 else "l1_wp"
        bq = self.bias_s[pre + "bq"]
        bk = self.bias_s[pre + "bk"]
        bo = self.bias_s["l0_bo" if L == 0 else "l1_bp"]
        tk = self.tok
        b01 = [self.abank[0], self.abank[1]]
        bav = [self.abank[2], self.abank[3]]
        bsc2 = self.psc2
        b6, b7 = self.abank[2], self.abank[3]
        blr = self.abank[0]

        self.load_wslot(0, i[wk_nm], (DCH, D))
        self.load_wslot(1, i[wv_nm], (DCH, D))
        self.load_wslot(2, i[wq_nm], (DCH, D))
        tok_bvr = dma("gpsimd", lambda gp: gp.dma_start(out=SN.bvrow[:, :], in_=i[pre + "bv"]),
                      self.p_w)
        for half in range(2):
            see("tensor", tok_bvr)
            see("tensor", b6.free_tok)
            b6.mm_tok = sig("tensor", lambda te, half=half: te.matmul(
                b6.t[:, 0:QCW], self.ones_rb[0:1, :], SN.bvrow[0:1, half * QCW:(half + 1) * QCW],
                start=True, stop=True))
            see("scalar", b6.mm_tok)
            b6.free_tok = sig("scalar", lambda se, half=half: se.activation(
                SN.bvb[:, half * QCW:(half + 1) * QCW], b6.t[:, 0:QCW], AF.Copy))
        tok_bvb = b6.free_tok
        tok_vp1 = sig("vector", lambda v: v.memset(
            SN.Vp[:, :].rearrange("p (k c) -> p k c", c=65)[:, :, 64:65], 1.0))
        for tc in range(NQC):
            win = slice(tc * QCW, (tc + 1) * QCW)
            par = tc % 2
            xch = SN.xn_ch[par]
            see("sync", tk.get(("xnch_fr", par)))
            self.dma("sync", lambda sy, s=s, tc=tc, xch=xch: sy.dma_start(
                out=xch[:, :], in_=self.xn_scr[s, tc]), self.p_xn_ld)
            W("tensor", self.p_xn_ld)
            W("tensor", self.p_w, self.wslot_tok[0][1])
            for j in range(DCH):
                bk_ = b01[j % 2]
                see("tensor", bk_.free_tok)
                for d in range(DCH):
                    fn = lambda te, j=j, d=d, bk_=bk_, xch=xch: te.matmul(
                        bk_.t[:, 0:QCW], self.wslot[0][:, d * D + j * 128:d * D + j * 128 + 128],
                        xch[:, d * QCW:(d + 1) * QCW],
                        start=(d == 0), stop=(d == DCH - 1))
                    if d == DCH - 1:
                        bk_.mm_tok = sig("tensor", fn)
                    else:
                        E_("tensor", fn)
                if tc == NQC - 1 and j == DCH - 1:
                    self.wslot_free[0] = bk_.mm_tok
                see("scalar", bk_.mm_tok)
                see("scalar", tk.get("kbf_fr"))
                tok_ke = sig("scalar", lambda se, j=j, bk_=bk_: se.activation(
                    SN.k_bf[:, :], bk_.t[:, 0:QCW], AF.Identity, bias=bk[:, j:j + 1]))
                bk_.free_tok = tok_ke
                dst = SN.KT[:, j * T + tc * QCW:j * T + (tc + 1) * QCW]
                t3 = self.rope(SN.k_bf, dst, win, tok_ke, None)
                tk["kbf_fr"] = t3
            W("tensor", self.p_w, self.wslot_tok[1][1])
            for tt in range(4):
                ktile = tc * 4 + tt
                toks_h = []
                for half in range(2):
                    bk_ = [b6, b7][half]
                    see("tensor", bk_.free_tok)
                    for d in range(DCH):
                        fn = lambda te, tt=tt, half=half, d=d, bk_=bk_, xch=xch: te.matmul(
                            bk_.t[:, 0:QCW],
                            xch[:, d * QCW + tt * 128:d * QCW + tt * 128 + 128],
                            self.wslot[1][:, d * D + half * QCW:d * D + (half + 1) * QCW],
                            start=(d == 0), stop=(d == DCH - 1))
                        if d == DCH - 1:
                            bk_.mm_tok = sig("tensor", fn)
                        else:
                            E_("tensor", fn)
                    see("vector", bk_.mm_tok)
                    see("vector", tok_bvb)
                    see("vector", tok_vp1)
                    base = ktile * (H * 65) + (8 * half) * 65
                    vout = SN.Vp[:, base:base + 8 * 65].rearrange("p (h c) -> p h c", c=65)[:, :, 0:64]
                    tok_v = sig("vector", lambda v, half=half, bk_=bk_, vout=vout: v.tensor_add(
                        vout, bk_.t[:, 0:QCW].rearrange("p (h c) -> p h c", c=64),
                        SN.bvb[:, half * QCW:(half + 1) * QCW].rearrange("p (h c) -> p h c", c=64)))
                    bk_.free_tok = tok_v
                    toks_h.append(tok_v)
                tok_vp = toks_h[-1]
                tk["vbf_fr"] = tok_vp
            tk[("xnch_fr", par)] = bk_.mm_tok
            if tc == NQC - 1:
                self.wslot_free[1] = tk["vbf_fr"]
        self.load_wslot(1, i[wo_nm], (DCH, D))

        tok_last_kt = tk.get("vbf_fr")
        for qc in range(NQC):
            win = slice(qc * QCW, (qc + 1) * QCW)
            par = qc % 2
            xch = SN.xn_ch[par]
            see("sync", tk.get(("xnch_fr", par)))
            self.dma("sync", lambda sy, s=s, qc=qc, xch=xch: sy.dma_start(
                out=xch[:, :], in_=self.xn_scr[s, qc]), self.p_xn_ld)
            W("tensor", self.p_xn_ld)
            W("tensor", self.p_w, self.wslot_tok[2][1])
            W("vector", self.p_ao)
            for j in range(DCH):
                bk_ = b01[j % 2]
                see("tensor", bk_.free_tok)
                see("tensor", tk.get("qtc_fr"))
                for d in range(DCH):
                    fn = lambda te, j=j, d=d, bk_=bk_, xch=xch: te.matmul(
                        bk_.t[:, 0:QCW], self.wslot[2][:, d * D + j * 128:d * D + j * 128 + 128],
                        xch[:, d * QCW:(d + 1) * QCW],
                        start=(d == 0), stop=(d == DCH - 1))
                    if d == DCH - 1:
                        bk_.mm_tok = sig("tensor", fn)
                    else:
                        E_("tensor", fn)
                if qc == NQC - 1 and j == DCH - 1:
                    self.wslot_free[2] = bk_.mm_tok
                see("vector", bk_.mm_tok)
                see("vector", tk.get("qbf_fr"))
                tok_qe = sig("vector", lambda v, j=j, bk_=bk_: v.tensor_scalar_add(
                    SN.q_bf[:, :], bk_.t[:, 0:QCW], bq[:, j:j + 1]))
                bk_.free_tok = tok_qe
                dst = SN.QTc[:, j * QCW:(j + 1) * QCW]
                t3 = self.rope(SN.q_bf, dst, win, tok_qe, tk.get("qtc_fr"))
                tk["qbf_fr"] = t3
            tok_q_ready = tk["qbf_fr"]
            tk[("xnch_fr", par)] = bk_.mm_tok
            tok_wo_prev = tk.get("och_fr")
            for hp in range(H // 2):
                nkt = NKT if L == 0 else 4 * qc + 4
                bvA = bav[0]
                bvB = bav[1]
                see("tensor", tok_q_ready)
                see("tensor", bvA.free_tok)
                see("tensor", bvB.free_tok)
                def sc_kt(kt):
                    bsp = bsc2[kt % 2]
                    see("tensor", bsp.free_tok)
                    for half in range(2):
                        fn = lambda te, kt=kt, half=half, bsp=bsp, hp=hp: te.matmul(
                            bsp.t[:, half * QCW:(half + 1) * QCW],
                            SN.KT[64 * half:64 * half + 64, hp * T + kt * 128:hp * T + kt * 128 + 128],
                            SN.QTc[64 * half:64 * half + 64, hp * QCW:(hp + 1) * QCW],
                            start=True, stop=True)
                        if half == 1:
                            bsp.mm_tok = sig("tensor", fn)
                        else:
                            E_("tensor", fn)
                def exp_kt(kt):
                    p = kt % 2
                    bsp = bsc2[p]
                    see("scalar", bsp.mm_tok)
                    see("scalar", tk.get(("exp_fr", p)))
                    tok_exp = sig("scalar", lambda se, p=p, bsp=bsp: se.activation(
                        SN.exp2[p][:, :], bsp.t[:, :], AF.Exp, scale=0.125))
                    bsp.free_tok = tok_exp
                    if L == 1 and kt >= 4 * qc:
                        m = kt - 4 * qc
                        see("vector", tok_exp)
                        tok_exp = sig("vector", lambda v, p=p, m=m: v.tensor_mul(
                            SN.exp2[p][:, :], SN.exp2[p][:, :],
                            SN.maskb[:, m * 1024:(m + 1) * 1024]))
                    return tok_exp
                def av_kt(kt, tok_exp):
                    p = kt % 2
                    see("tensor", tok_exp)
                    tok_av = None
                    for half in range(2):
                        h = 2 * hp + half
                        bv_ = (bvA, bvB)[half]
                        fn = lambda te, kt=kt, p=p, h=h, half=half, bv_=bv_, nkt=nkt: te.matmul(
                            bv_.t[0:65, 0:QCW],
                            SN.Vp[:, kt * (H * 65) + h * 65:kt * (H * 65) + h * 65 + 65],
                            SN.exp2[p][:, half * QCW:(half + 1) * QCW],
                            start=(kt == 0), stop=(kt == nkt - 1))
                        if half == 1:
                            tok_av = sig("tensor", fn)
                        else:
                            E_("tensor", fn)
                    tk[("exp_fr", p)] = tok_av
                    return tok_av
                sc_kt(0)
                etoks = {0: exp_kt(0)}
                for kt in range(nkt):
                    if kt + 1 < nkt:
                        sc_kt(kt + 1)
                        etoks[kt + 1] = exp_kt(kt + 1)
                    tok_av = av_kt(kt, etoks.pop(kt))
                bvA.mm_tok = bvB.mm_tok = tok_av
                pp2 = hp % 2
                for half in range(2):
                    bv_ = (bvA, bvB)[half]
                    see("vector", tok_av)
                    see("vector", tk.get(("oraw_fr", pp2, half)))
                    t_ev = sig("vector", lambda v, half=half, pp2=pp2, bv_=bv_: v.tensor_copy(
                        SN.oraw[0:65, (2 * pp2 + half) * QCW:(2 * pp2 + half + 1) * QCW],
                        bv_.t[0:65, 0:QCW]))
                    bv_.free_tok = t_ev
                    tk[("oraw_rdy", pp2, half)] = t_ev

                def norm_emit(hp2):
                    pp3 = hp2 % 2
                    for half in range(2):
                        h = 2 * hp2 + half
                        col = (2 * pp3 + half) * QCW
                        see("scalar", tk.get(("oraw_rdy", pp3, half)))
                        see("scalar", tk.get(("lrow_fr", half)))
                        t_l = sig("scalar", lambda se, half=half, col=col: se.activation(
                            SN.lrow[half][:, :], SN.oraw[64:65, col:col + QCW], AF.Ln))
                        see("scalar", t_l)
                        t_r = sig("scalar", lambda se, half=half: se.activation(
                            SN.lrinv[half][:, :], SN.lrow[half][:, :], AF.Exp, scale=-1.0))
                        tk[("lrow_fr", half)] = t_r
                        blr_ = b01[half]
                        see("tensor", t_r)
                        see("tensor", blr_.free_tok)
                        blr_.mm_tok = sig("tensor", lambda te, half=half, blr_=blr_: te.matmul(
                            blr_.t[0:64, 0:QCW], self.ones_rb[0:1, 0:64],
                            SN.lrinv[half][0:1, :], start=True, stop=True))
                        see("vector", blr_.mm_tok)
                        see("vector", tk.get(("linvb_fr", half)))
                        tok_lb = sig("vector", lambda v, half=half, blr_=blr_: v.tensor_copy(
                            SN.linv_b[:, half * QCW:(half + 1) * QCW],
                            blr_.t[0:64, 0:QCW]))
                        blr_.free_tok = tok_lb
                        see("vector", tok_lb)
                        see("vector", tok_wo_prev)
                        tok_on = sig("vector", lambda v, half=half, hp2=hp2, col=col: v.tensor_mul(
                            SN.och[64 * half:64 * half + 64, hp2 * QCW:(hp2 + 1) * QCW],
                            SN.oraw[0:64, col:col + QCW],
                            SN.linv_b[0:64, half * QCW:(half + 1) * QCW]))
                        tk[("linvb_fr", half)] = tok_on
                        tk[("oraw_fr", pp3, half)] = tok_on
                        tk[("och", h)] = tok_on
                if hp > 0:
                    norm_emit(hp - 1)
                if hp == H // 2 - 1:
                    norm_emit(hp)
            W("tensor", self.p_w, self.wslot_tok[1][1])
            for h in range(H):
                see("tensor", tk[("och", h)])
            tok_wo_last = None
            tok_evac_last = None
            for i2 in range(DCH):
                bk_ = b01[i2 % 2]
                see("tensor", bk_.free_tok)
                for j in range(DCH):
                    fn = lambda te, i2=i2, j=j, bk_=bk_: te.matmul(
                        bk_.t[:, 0:QCW], self.wslot[1][:, j * D + i2 * 128:j * D + i2 * 128 + 128],
                        SN.och[:, j * QCW:(j + 1) * QCW],
                        start=(j == 0), stop=(j == DCH - 1))
                    if j == DCH - 1:
                        bk_.mm_tok = sig("tensor", fn)
                    else:
                        E_("tensor", fn)
                tok_wo_last = bk_.mm_tok
                see("vector", bk_.mm_tok)
                W("vector", self.p_ao)
                tok_ev = sig("vector", lambda v, i2=i2, bk_=bk_: v.tensor_scalar_add(
                    SN.QTc[:, i2 * QCW:(i2 + 1) * QCW], bk_.t[:, 0:QCW],
                    bo[:, i2:i2 + 1]))
                bk_.free_tok = tok_ev
                tok_evac_last = tok_ev
            tk["och_fr"] = tok_wo_last
            tk["qtc_fr"] = tok_wo_last
            see("sync", tok_evac_last)
            dma("sync", lambda sy, qc=qc: sy.dma_start(out=self.ao_scr[qc], in_=SN.QTc[:, :]),
                self.p_ao)
        self.wslot_free[1] = tk["och_fr"]

    def phase_ffn(self, L, s):
        SN = self.snap("xn_s acc att_tk xres_tk m_buf wbc_sb sg_sb su_sb selb")
        E_, sig, see, W, dma = self.E, self.sig, self.see, self.W, self.dma
        i = self.i
        tk = self.tok
        Elay = 8 if L == 0 else 16
        wT = self.w0T[s] if L == 0 else self.w1T[s]
        gu_nm = "e0_gu" if L == 0 else "e1_gu"
        d_nm = "e0_d" if L == 0 else "e1_d"
        pg_b = [self.fbank[0], self.fbank[2]]
        pu_b = [self.fbank[1], self.fbank[3]]
        dacc = self.fbank[4:8]

        for tc in range(NQC):
            out = SN.xn_s[:, :].rearrange("p (c t) -> p c t", c=DCH)[:, :, tc * QCW:(tc + 1) * QCW]
            in_ = self.xn_scr[s, tc].rearrange("p (c t) -> p c t", c=DCH)
            dma("sync", lambda sy, out=out, in_=in_: sy.dma_start(out=out, in_=in_), self.p_xn_ld)
        W("tensor", self.p_xn_ld)

        groups = [[e, e + 1] for e in range(0, Elay, 2)] + [["sh"]]
        acc_tok = [None] * DCH
        tok_m_fr = [None, None]
        for gi, group in enumerate(groups):
            shared = group[0] == "sh"
            itn = 8 if (shared and L == 0) else 4
            if shared:
                if L == 0:
                    tok_g = self.load_wslot(0, i["s0_g"], (DCH, D))
                    tok_u = self.load_wslot(1, i["s0_u"], (DCH, D))
                    see("gpsimd", self.wslot_free[2])
                    tok_d0 = self.load_wslot(2, i["s0_d"], (DCH, D))
                    gu_toks = [(tok_g, tok_u)]
                    d_toks = [tok_d0]
                else:
                    gu_toks = [(self.load_wslot(0, i["s1_gu"], (2, DCH, I_EXP)),) * 2]
                    d_toks = [self.load_whalf(2, 0, i["s1_d"], self.wslot_free[2])]
            else:
                gu_toks = []
                d_toks = []
                for k, e in enumerate(group):
                    t1 = self.load_wslot(k, i[gu_nm][e], (2, DCH, I_EXP))
                    gu_toks.append((t1, t1))
                    d_toks.append(self.load_whalf(2, k, i[d_nm][e],
                                                  self.wslot_free[2] if k == 0 else None))
            FW = 2 * TKW
            for tkc in range(NQC):
                if not shared:
                    for k, e in enumerate(group):
                        bb = pg_b[k % 2]
                        see("tensor", bb.free_tok)
                        bb.mm_tok = sig("tensor", lambda te, e=e, tkc=tkc, bb=bb: te.matmul(
                            bb.t[:, 0:FW], SN.selb[0:Elay, e * 128:(e + 1) * 128],
                            wT[0:Elay, tkc * FW:(tkc + 1) * FW], start=True, stop=True))
                        see("scalar", bb.mm_tok)
                        see("scalar", tk.get("wu_fr"))
                        tok_w = sig("scalar", lambda se, k=k, bb=bb: se.activation(
                            SN.wbc_sb[:, k * FW:(k + 1) * FW],
                            bb.t[:, 0:FW], AF.Copy))
                        bb.free_tok = tok_w
                        tk[("wbc", k)] = tok_w
                def m_ap(k, it):
                    if shared and L == 0:
                        return SN.m_buf[it // 4][:, (it % 4) * FW:(it % 4) * FW + FW]
                    return SN.m_buf[k][:, it * FW:(it + 1) * FW]
                tok_m_done = [None, None]
                last_f1 = None
                for k, u in enumerate(group):
                    for it in range(itn):
                        pg = pg_b[it % 2]
                        pu = pu_b[it % 2]
                        see("tensor", pg.free_tok)
                        see("tensor", pu.free_tok)
                        if tkc == 0 and it == 0:
                            see("tensor", gu_toks[k][0] if not (shared and L == 0) else gu_toks[0][0])
                            if shared and L == 0:
                                see("tensor", gu_toks[0][1])
                        for d in range(DCH):
                            if shared and L == 0:
                                g_ap = lambda d=d, it=it: self.wslot[0][:, d * D + it * 128:d * D + it * 128 + 128]
                                u_ap = lambda d=d, it=it: self.wslot[1][:, d * D + it * 128:d * D + it * 128 + 128]
                            else:
                                g_ap = lambda d=d, it=it, k=k: self.wslot[k][:, d * I_EXP + it * 128:d * I_EXP + it * 128 + 128]
                                u_ap = lambda d=d, it=it, k=k: self.wslot[k][:, 4096 + d * I_EXP + it * 128:4096 + d * I_EXP + it * 128 + 128]
                            rhs = lambda d=d, tkc=tkc: SN.xn_s[:, d * T + tkc * FW:d * T + (tkc + 1) * FW]
                            fg = lambda te, g_ap=g_ap, rhs=rhs, d=d, pg=pg: te.matmul(
                                pg.t[:, 0:FW], g_ap(), rhs(), start=(d == 0), stop=(d == DCH - 1))
                            fu = lambda te, u_ap=u_ap, rhs=rhs, d=d, pu=pu: te.matmul(
                                pu.t[:, 0:FW], u_ap(), rhs(), start=(d == 0), stop=(d == DCH - 1))
                            if d == DCH - 1:
                                pg.mm_tok = sig("tensor", fg)
                                pu.mm_tok = sig("tensor", fu)
                                last_f1 = pu.mm_tok
                            else:
                                E_("tensor", fg)
                                E_("tensor", fu)
                        see("scalar", pg.mm_tok)
                        see("scalar", tk.get("sg_fr"))
                        tok_sg = sig("scalar", lambda se, pg=pg: se.activation(
                            SN.sg_sb[:, :], pg.t[:, 0:FW], AF.Silu))
                        pg.free_tok = tok_sg
                        see("vector", pu.mm_tok)
                        see("vector", tk.get("su_fr"))
                        if not shared:
                            see("vector", tk.get(("wbc", k)))
                            tok_su = sig("vector", lambda v, pu=pu, k=k: v.tensor_mul(
                                SN.su_sb[:, :], pu.t[:, 0:FW],
                                SN.wbc_sb[:, k * FW:(k + 1) * FW]))
                        else:
                            tok_su = sig("vector", lambda v, pu=pu: v.tensor_copy(
                                SN.su_sb[:, :], pu.t[:, 0:FW]))
                        pu.free_tok = tok_su
                        see("vector", tok_sg)
                        see("vector", tok_su)
                        see("vector", tok_m_fr[k if not (shared and L == 0) else it // 4])
                        tok_m = sig("vector", lambda v, ap=m_ap(k, it): v.tensor_mul(
                            ap, SN.sg_sb[:, :], SN.su_sb[:, :]))
                        tk["sg_fr"] = tok_m
                        tk["su_fr"] = tok_m
                        tk["wu_fr"] = tok_m
                        tok_m_done[k] = tok_m
                tok_m_fr = tok_m_done
                for k in range(len(group)):
                    see("tensor", tok_m_done[k])
                if tkc == 0:
                    for td in d_toks:
                        see("tensor", td)
                tok_f2 = [None] * DCH
                for hp in range(2):
                    for i2 in range(hp * 4, hp * 4 + 4):
                        bb = dacc[i2 % 4]
                        see("tensor", acc_tok[i2])
                        nmm = len(group) * itn
                        c = 0
                        for k in range(len(group)):
                            for it in range(itn):
                                if shared and L == 0:
                                    d_ap = lambda it=it, i2=i2: self.wslot[2][:, it * D + i2 * 128:it * D + i2 * 128 + 128]
                                else:
                                    d_ap = lambda it=it, i2=i2, k=k: self.wslot[2][:, k * 4096 + it * D + i2 * 128:k * 4096 + it * D + i2 * 128 + 128]
                                fn = lambda te, d_ap=d_ap, m=m_ap(k, it), bb=bb, c=c, nmm=nmm: te.matmul(
                                    bb.t[:, 0:FW], d_ap(), m,
                                    start=(c == 0), stop=(c == nmm - 1))
                                if c == nmm - 1:
                                    tok_f2[i2] = sig("tensor", fn)
                                else:
                                    E_("tensor", fn)
                                c += 1
                    for i2 in range(hp * 4, hp * 4 + 4):
                        bb = dacc[i2 % 4]
                        see("vector", tok_f2[i2])
                        see("vector", acc_tok[i2])
                        aslice = SN.acc[:, i2 * T + tkc * FW:i2 * T + (tkc + 1) * FW]
                        if gi == 0:
                            acc_tok[i2] = sig("vector", lambda v, aslice=aslice, bb=bb: v.tensor_copy(
                                aslice, bb.t[:, 0:FW]))
                        else:
                            acc_tok[i2] = sig("vector", lambda v, aslice=aslice, bb=bb: v.tensor_add(
                                aslice, aslice, bb.t[:, 0:FW]))
            self.wslot_free[0] = last_f1
            self.wslot_free[1] = last_f1
            self.wslot_free[2] = tok_f2[DCH - 1]

        tok_cmb_prev = None
        tok_out_prev = [None, None]
        for tkc in range(NTK):
            par = 0
            tc = tkc // 2
            half = tkc % 2
            see("sync", tok_cmb_prev)
            if L == 0:
                src = self.i["xT"][s].rearrange("(c p) t -> p c t", p=128)[:, :, tkc * TKW:(tkc + 1) * TKW]
            else:
                src = self.x1_scr[s, tkc].rearrange("p (c u) -> p c u", c=DCH)
            t_res = dma("sync", lambda sy, src=src, par=par: sy.dma_start(
                out=SN.xres_tk[par][:, :].rearrange("p (c t) -> p c t", c=DCH), in_=src),
                self.p_res)
            aoin = self.ao_scr[tc].rearrange("p (i q) -> p i q", i=DCH)[:, :, half * TKW:(half + 1) * TKW]
            t_ao = dma("sync", lambda sy, aoin=aoin, par=par: sy.dma_start(
                out=SN.att_tk[par][:, :].rearrange("p (c t) -> p c t", c=DCH), in_=aoin),
                self.p_ao)
            accv = SN.acc[:, :].rearrange("p (c t) -> p c t", c=DCH)[:, :, tkc * TKW:(tkc + 1) * TKW]
            attv = SN.att_tk[par][:, :].rearrange("p (c t) -> p c t", c=DCH)
            resv = SN.xres_tk[par][:, :].rearrange("p (c t) -> p c t", c=DCH)
            xnv = SN.xn_s[:, :].rearrange("p (c t) -> p c t", c=DCH)[:, :, tkc * TKW:(tkc + 1) * TKW]
            W("vector", self.p_ao)
            W("vector", self.p_res)
            for i2 in range(DCH):
                see("vector", acc_tok[i2])
            t_c1 = sig("vector", lambda v, accv=accv, attv=attv: v.tensor_add(accv, accv, attv))
            see("vector", t_c1)
            t_c2 = sig("vector", lambda v, accv=accv, resv=resv: v.tensor_add(accv, accv, resv))
            if L == 1:
                see("vector", t_c2)
                t_c2 = sig("vector", lambda v, accv=accv, xnv=xnv: v.tensor_add(accv, accv, xnv))
            tok_cmb_prev = t_c2
            see("sync", t_c2)
            if L == 0:
                dst = self.x1_scr[s, tkc]
                out_ap = SN.acc[:, :].rearrange("p (c t) -> p c t", c=DCH)[:, :, tkc * TKW:(tkc + 1) * TKW]
                tok_st = dma("sync", lambda sy, dst=dst, out_ap=out_ap: sy.dma_start(
                    out=dst.rearrange("p (c t) -> p c t", c=DCH), in_=out_ap), self.p_out)
            else:
                dst = self.out[s].rearrange("(c p) t -> p c t", p=128)[:, :, tkc * TKW:(tkc + 1) * TKW]
                out_ap = SN.acc[:, :].rearrange("p (c t) -> p c t", c=DCH)[:, :, tkc * TKW:(tkc + 1) * TKW]
                tok_st = dma("sync", lambda sy, dst=dst, out_ap=out_ap: sy.dma_start(
                    out=dst, in_=out_ap), self.p_out)
            tok_out_prev[par] = tok_st

    def finish(self):
        E, sig, see, W = self.E, self.sig, self.see, self.W
        self.quiesce()
        for nm in list(self.dbg):
            src = {"w0T": lambda: self.w0T[0], "w0T1": lambda: self.w0T[1],
                   "wmean": lambda: self.wmean, "stats": lambda: self.stats_gl,
                   "stats_loc": lambda: self.stats_loc}.get(nm)
            if src is None:
                continue
            t = src()
            self.dma("gpsimd", lambda g, t=t, nm=nm: g.dma_start(out=self.dbg[nm], in_=t[:, :]),
                     self.p_dbg)
        W("gpsimd", self.p_dbg)
        W("sync", self.p_out)
        W("sync", self.p_xn_st)

def build_program(debug_names=(), stage="prepass", serialize=False):
    nc = bass.Bass("TRN2", target_bir_lowering=False, detect_race_conditions=False)
    if os.environ.get("BIG_SBUF") == "1":
        nc.sbuf_top = 4_000_000
    st = ExitStack()
    with st:
        b = Build(nc, st, debug_names, serialize=serialize)
        b.declare_io()
        if "w0T" in b.debug_names:
            b.dout("w0T", [16, T]); b.dout("w0T1", [16, T])
        if "wmean" in b.debug_names:
            b.dout("wmean", [8, 1])
        if "stats" in b.debug_names:
            b.dout("stats", [8, 2])
        for nm, shape, dt in [("xb", [128, 4096], BF16), ("xq", [128, 4096], BF16),
                              ("mrow", [1, 512], F32), ("erow", [1, 512], F32),
                              ("vrow", [1, 512], F32), ("rrow", [1, 512], F32),
                              ("sct", [16, 512], F32), ("stats_loc", [128, 32], F32),
                              ("xnb", [128, 4096], BF16), ("xc0", [128, 4096], F32)]:
            if nm in b.debug_names:
                b.dout(nm, shape, dt)
        b.alloc()
        if os.environ.get("BIG_SBUF") == "1":
            probe = st.enter_context(nc.sbuf_tensor("addrprobe", [128, 32], mybir.dt.uint8))
            base = nc.lookup_mloc(probe).addr + 32
            st.enter_context(nc.sbuf_tensor("bigpad", [128, 229368 - base], mybir.dt.uint8))
        b.load_consts()
        nseq = int(os.environ.get("PP_SEQS", str(S)))
        b.open_scope(b.alloc_pre)
        for s in range(nseq):
            b.prepass_seq(0, s)
        if os.environ.get("SKIP_CC") != "1" and os.environ.get("SKIP_ROUTER") != "1" and nseq == S:
            b.collective_wmean()
        b.close_scope()
        nlayer = int(os.environ.get("STAGE_LAYERS", "2"))
        for L in range(nlayer):
            if L == 1:
                b.open_scope(b.alloc_pre)
                for s in range(S):
                    b.prepass_seq(1, s)
                b.close_scope()
            for s in range(S):
                b.open_scope(b.alloc_att)
                b.phase_att(L, s)
                b.close_scope()
                b.open_scope(b.alloc_ffn)
                b.phase_ffn(L, s)
                b.close_scope()
        b.finish()
        if os.environ.get("BIG_SBUF") == "1":
            while getattr(b, "_scopes", []):
                b._scopes.pop().close()
        with nc.Block() as block:
            @block.sync
            def _(sy):
                b.drain("sync", sy)
            @block.gpsimd
            def _(g):
                b.drain("gpsimd", g)
            @block.tensor
            def _(te):
                b.drain("tensor", te)
            @block.scalar
            def _(se):
                b.drain("scalar", se)
            @block.vector
            def _(ve):
                b.drain("vector", ve)
    return nc, b


def kernel(**inputs):
    g = host_layout(inputs)
    nc, b = build_program()
    in_maps = []
    for c in range(NCORES):
        m = dict(g)
        m["xT"] = g["xT"][2 * c:2 * c + 2]
        in_maps.append(m)
    res = run_bass_kernel_spmd(nc, in_maps, core_ids=list(range(NCORES)))
    outs = [np.asarray(r["out"], dtype=np.float32) for r in res.results]
    full = np.concatenate(outs, axis=0)
    return np.ascontiguousarray(full.transpose(0, 2, 1))
